# version 05
# baseline (speedup 1.0000x reference)
import numpy as np
import ml_dtypes
import concourse.bass as bass
import concourse.mybir as mybir
from concourse.bass_utils import run_bass_kernel_spmd

F32 = mybir.dt.float32
BF16 = mybir.dt.bfloat16
U32 = mybir.dt.uint32
AF = mybir.ActivationFunctionType
ALU = mybir.AluOpType

P = 128
D = 2048
KT = D // P
NH = 16
FF = 1024
FT = FF // P
E = 8
NEGLN2 = -0.6931471805599453
EPS = 1e-5
TINY = 1.1754943508222875e-38

ENGINES = ("pe", "dve", "act", "pool", "sp")


class Prog:

    def __init__(self, nc):
        self.nc = nc
        self.sems = {}
        self.counts = {}
        self.steps = {e: [] for e in ENGINES}
        names = ["dS", "dN", "dX", "pe", "dve", "act", "pool", "sp"]
        names += [f"dG{i}" for i in range(4)] + [f"dR{i}" for i in range(4)]
        names += [f"dW{i}" for i in range(4)] + [f"dA{i}" for i in range(2)]
        names += [f"out{i}" for i in range(4)]
        for s in names:
            self.sems[s] = nc.alloc_semaphore(s)
            self.counts[s] = 0

    def step(self, eng, action, deps=(), inc=None, inc_by=1):
        tok = None
        if inc is not None:
            self.counts[inc] += inc_by
            tok = (inc, self.counts[inc])
        self.steps[eng].append(
            (tuple(d for d in deps if d is not None), action, inc, inc_by))
        return tok

    def tick(self, eng, action, deps=()):
        return self.step(eng, action, deps, inc=eng)

    def dma(self, queue, action, deps=(), sem="dS"):
        return self.step(queue, action, deps, inc=sem, inc_by=16)

    def barrier(self):
        toks = [(e, self.counts[e]) for e in ENGINES]
        dma_toks = [(sname, self.counts[sname]) for sname in self.sems
                    if sname not in ENGINES]
        for e in ENGINES:
            deps = [(e, self.counts[e])] + \
                [t for t in toks if t[0] != e] + dma_toks
            self.tick(e, lambda en: en.nop(), deps)

    def emit(self):
        nc = self.nc

        def runner(eng):
            def run(engobj):
                lastwait = {}
                for deps, action, inc, inc_by in self.steps[eng]:
                    for s, v in deps:
                        if lastwait.get(s, 0) < v:
                            engobj.wait_ge(self.sems[s], v)
                            lastwait[s] = v
                    ins = action(engobj)
                    if inc is not None:
                        ins.then_inc(self.sems[inc], inc_by)
            return run

        with nc.Block() as block:
            block.sync(runner("sp"))
            block.gpsimd(runner("pool"))
            block.scalar(runner("act"))
            block.vector(runner("dve"))
            block.tensor(runner("pe"))


class Buf:

    def __init__(self):
        self.ready = None
        self.readers = []

    def begin_write(self):
        d = self.readers + ([self.ready] if self.ready else [])
        self.readers = []
        return d

    def wrote(self, tok):
        self.ready = tok

    def read(self, tok):
        self.readers.append(tok)
        return self.ready


class Ring:

    def __init__(self, n):
        self.bufs = [Buf() for _ in range(n)]
        self.i = 0

    def next(self):
        s = self.i % len(self.bufs)
        self.i += 1
        return s, self.bufs[s]


def build(TC=2048, PREF=6144, CW=512):
    TH = TC // 2
    CW = min(CW, TH)
    NCW = TH // CW
    assert NCW in (1, 2)
    PT = PREF // P
    TT = TH // P

    nc = bass.Bass("TRN2", target_bir_lowering=False)
    pr = Prog(nc)

    dt = nc.dram_tensor
    xfm_d = dt("xfm", [D, TC], F32, kind="ExternalInput").ap()
    xpre_d = dt("xpre", [PREF, D], F32, kind="ExternalInput").ap()
    wq_d = dt("wqT", [KT, P, KT, P], F32, kind="ExternalInput").ap()
    wv_d = dt("wvT", [KT, P, KT, P], F32, kind="ExternalInput").ap()
    wg_d = dt("wgT", [KT, P, KT, P], F32, kind="ExternalInput").ap()
    wo_d = dt("woT", [KT, P, KT, P], F32, kind="ExternalInput").ap()
    w1_d = dt("w1T", [E, FT, P, KT, P], F32, kind="ExternalInput").ap()
    w3_d = dt("w3T", [E, FT, P, KT, P], F32, kind="ExternalInput").ap()
    w2_d = dt("w2T", [KT, P, E, FT, P], F32, kind="ExternalInput").ap()
    wgf_d = dt("wgfT", [P, KT, E], F32, kind="ExternalInput").ap()
    lngam_d = dt("lngam", [P, KT], F32, kind="ExternalInput").ap()
    lnbet_d = dt("lnbet", [P, KT], F32, kind="ExternalInput").ap()
    hbeta_d = dt("hbeta", [NH, 1], F32, kind="ExternalInput").ap()
    fbias_d = dt("fbias", [E, 1], F32, kind="ExternalInput").ap()
    cexp_d = dt("cexp", [P, PT], F32, kind="ExternalInput").ap()
    out_d = dt("outfm", [D, TC], F32, kind="ExternalOutput").ap()
    x1hbm = dt("x1hbm", [D, TC], F32, kind="Internal").ap()

    A = nc.alloc_sbuf_tensor

    lngamS = A("lngamS", [P, KT], F32)
    lnbetS = A("lnbetS", [P, KT], F32)
    hbetaS = A("hbetaS", [NH, 1], F32)
    fbiasS = A("fbiasS", [E, 1], F32)
    identf = A("identf", [P, P], F32)
    iota8f = A("iota8f", [P, E], F32)
    e8f = A("e8f", [E, E * P], F32)
    onescol_bf = A("onescol_bf", [P, 1], BF16)
    onesrow_f = A("onesrow_f", [1, P], F32)
    lam16 = A("lam16", [NH, 1], F32)
    loglam16 = A("loglam16", [NH, 1], F32)
    t16 = A("t16", [NH, 1], F32)
    lamP = A("lamP", [P, NH], F32)
    loglamP = A("loglamP", [P, NH], F32)
    rowP = A("rowP", [1, NH], F32)
    fbB = A("fbB", [P, NH], F32)
    ZTf = A("ZTf", [P, KT * NH], F32)
    ZTb = A("ZTb", [P, KT * NH], BF16)
    carryS = A("carryS", [P, NH], F32)
    stateS = A("stateS", [P, NH], F32)
    wgfS = A("wgfS", [P, KT * E], BF16)
    muR = A("muR", [1, TH], F32)
    msR = A("msR", [1, TH], F32)
    rstdR = A("rstdR", [1, TH], F32)
    muB = A("muB", [P, TH], F32)
    rstdB = A("rstdB", [P, TH], F32)
    r8a = A("r8a", [E, TH], F32)
    ewTf = A("ewTf", [E, TH], F32)
    rt_m8 = A("rt_m8", [P, E], F32)
    rt_i8 = A("rt_i8", [P, E], U32)
    rt_i8f = A("rt_i8f", [P, E], F32)
    rt_oh1 = A("rt_oh1", [P, E], F32)
    rt_oh2 = A("rt_oh2", [P, E], F32)
    rt_jnk = A("rt_jnk", [P, E], F32)
    rt_w1 = A("rt_w1", [P, 1], F32)
    rt_w2 = A("rt_w2", [P, 1], F32)
    rt_ws = A("rt_ws", [P, 1], F32)
    sctS = A("sctS", [P, E], F32)
    ewtS = A("ewtS", [P, E], F32)
    ewbS = A("ewbS", [P, TH], F32)
    xnS = A("xnS", [P, KT * TH], BF16)
    xf32S = A("xf32S", [P, 4 * CW], F32)
    NWR = 4
    wringS = [A(f"wring{i}", [P, KT * P], BF16) for i in range(NWR)]

    psS = [nc.alloc_psum_tensor(f"ps{i}", [P, 512], F32) for i in range(8)]
    psB = [Buf() for _ in range(8)]

    muB_b = Buf()
    rstdB_b = Buf()
    rowsB = Buf()
    ewbB = Buf()
    carryB = [Buf() for _ in range(NH)]
    stateB = [Buf() for _ in range(NH)]
    wring = Ring(NWR)
    xn_b = Buf()
    vR = Ring(3)
    retR = Ring(3)
    sqR = Ring(2)
    cbR = Ring(2)
    ntR = Ring(2)
    xfR = Ring(4)
    x1R = Ring(4)
    sh1R = Ring(2)
    gtR = Ring(2)
    otR = Ring(4)
    x2B = [[Buf() for _ in range(NCW)] for _ in range(KT)]
    gbufB = [[Buf() for _ in range(FT)] for _ in range(2)]

    def load_wcol(src_ap):
        s, b = wring.next()
        X = int(np.prod(src_ap.shape[1:]))
        tok = pr.dma("pool",
                     lambda en, t=wringS[s], sr=src_ap, X=X: en.dma_start(
                         out=t[:, 0:X], in_=sr),
                     deps=b.begin_write(), sem=f"dG{s}")
        b.wrote(tok)
        return s, b, tok

    zscope = nc.sbuf_tensor("zscope", [P, PT + 3 * PT * NH // 2 + 2 * D], F32)
    zs_t = zscope.__enter__()
    zscope.__exit__(None, None, None)
    cexp_ctx = nc.sbuf_tensor("cexpS", [P, PT], F32)
    cexpS = cexp_ctx.__enter__()
    preW_ctx = nc.sbuf_tensor("preW", [P, PT * NH], F32)
    preW = preW_ctx.__enter__()
    preWb_ctx = nc.sbuf_tensor("preWb", [P, PT * NH], BF16)
    preWb = preWb_ctx.__enter__()
    ZS_ctx = nc.sbuf_tensor("ZS", [NH, D], F32)
    ZS = ZS_ctx.__enter__()

    pr.dma("sp", lambda en: en.dma_start(out=lngamS[:], in_=lngam_d))
    pr.dma("sp", lambda en: en.dma_start(out=lnbetS[:], in_=lnbet_d))
    pr.dma("sp", lambda en: en.dma_start(out=hbetaS[:], in_=hbeta_d))
    pr.dma("sp", lambda en: en.dma_start(out=fbiasS[:], in_=fbias_d))
    pr.dma("sp", lambda en: en.dma_start(out=cexpS[:], in_=cexp_d))
    t_setup = ("dS", pr.counts["dS"])
    t_lngam = t_lnbet = t_hbeta = t_fbias = t_cexp = t_setup
    t_wgf = pr.dma("pool", lambda en: en.dma_start(
        out=wgfS[:], in_=wgf_d.rearrange("p kt e -> p (kt e)")), sem="dX")

    t_id0 = pr.tick("pool", lambda en: en.iota(
        identf[:], pattern=[[1, P]], channel_multiplier=-1,
        allow_small_or_imprecise_dtypes=True))
    t_id = pr.tick("pool", lambda en: en.tensor_scalar(
        out=identf[:], in0=identf[:], scalar1=0.0, scalar2=None,
        op0=ALU.is_equal), deps=[t_id0])
    t_io8 = pr.tick("pool", lambda en: en.iota(
        iota8f[:], pattern=[[1, E]], channel_multiplier=0,
        allow_small_or_imprecise_dtypes=True))
    t_e80 = pr.tick("pool", lambda en: en.iota(
        e8f[:], pattern=[[1, E], [0, P]], channel_multiplier=-1,
        allow_small_or_imprecise_dtypes=True))
    t_e8 = pr.tick("pool", lambda en: en.tensor_scalar(
        out=e8f[:], in0=e8f[:], scalar1=0.0, scalar2=None,
        op0=ALU.is_equal), deps=[t_e80])
    t_ones_bf = pr.tick("dve", lambda en: en.memset(onescol_bf[:], 1.0))
    t_ones_f = pr.tick("dve", lambda en: en.memset(onesrow_f[:], 1.0))

    t_lpre = pr.tick("dve", lambda en: en.tensor_scalar(
        out=t16[:], in0=hbetaS[:], scalar1=NEGLN2, scalar2=None,
        op0=ALU.mult), deps=[t_hbeta])
    t_l0 = pr.tick("act", lambda en: en.activation(
        out=t16[:], in_=t16[:], func=AF.Exp), deps=[t_lpre])
    t_l1 = pr.tick("dve", lambda en: en.tensor_scalar(
        out=lam16[:], in0=t16[:], scalar1=-1.0, scalar2=1.0,
        op0=ALU.mult, op1=ALU.add), deps=[t_l0])
    t_l2 = pr.tick("dve", lambda en: en.tensor_scalar(
        out=lam16[:], in0=lam16[:], scalar1=1.0, scalar2=TINY,
        op0=ALU.min, op1=ALU.max), deps=[t_l1])
    t_l3 = pr.tick("act", lambda en: en.activation(
        out=loglam16[:], in_=lam16[:], func=AF.Ln), deps=[t_l2])

    def bcast16(src, dst, deps, n):
        d7 = psB[7].begin_write()
        tt_ = pr.tick("pe", lambda en, s=src, n=n: en.transpose(
            out=psS[7][0:1, 0:n], in_=s[:], identity=identf[0:n, 0:n]),
            deps=list(deps) + [t_id] + d7)
        psB[7].wrote(tt_)
        te = pr.tick("dve", lambda en, n=n: en.tensor_copy(
            out=rowP[:, 0:n], in_=psS[7][0:1, 0:n]), deps=[tt_])
        psB[7].read(te)
        d7 = psB[7].begin_write()
        tm = pr.tick("pe", lambda en, n=n: en.matmul(
            out=psS[7][:, 0:n], lhsT=onesrow_f[:], rhs=rowP[:, 0:n],
            start=True, stop=True), deps=[te, t_ones_f] + d7)
        psB[7].wrote(tm)
        tb = pr.tick("dve", lambda en, d=dst, n=n: en.tensor_copy(
            out=d[:, 0:n], in_=psS[7][:, 0:n]), deps=[tm])
        psB[7].read(tb)
        return tb

    t_lamP = bcast16(lam16, lamP, [t_l2], NH)
    t_loglamP = bcast16(loglam16, loglamP, [t_l3], NH)
    t_fbB = bcast16(fbiasS, fbB, [t_fbias], E)

    t_preW = []
    for h in range(NH):
        t = pr.tick("act", lambda en, h=h: en.activation(
            out=preW[:, h:h + (PT - 1) * NH + 1:NH], in_=cexpS[:],
            func=AF.Exp, scale=loglamP[:, h:h + 1]),
            deps=[t_cexp, t_loglamP])
        t_preW.append(t)
    t_preWb = pr.tick("dve", lambda en: en.tensor_copy(
        out=preWb[:], in_=preW[:]), deps=t_preW)

    ZCH = D // 512
    z_tok = [None] * ZCH
    for j in range(PT):
        slot, wb, tld = load_wcol(xpre_d[j * P:(j + 1) * P, :])
        last = None
        for c in range(ZCH):
            dep = [tld, t_preWb]
            if j == 0:
                dep += psB[c].begin_write()
            tok = pr.step("pe", lambda en, j=j, c=c, slot=slot: en.matmul(
                out=psS[c][0:NH, 0:512],
                lhsT=preWb[:, j * NH:(j + 1) * NH],
                rhs=wringS[slot][:, c * 512:(c + 1) * 512],
                start=(j == 0), stop=(j == PT - 1)),
                deps=dep, inc="pe" if c == ZCH - 1 else None)
            if j == PT - 1:
                z_tok[c] = tok
                if c == ZCH - 1:
                    psB[c].wrote(tok)
            last = tok
        for c in range(ZCH):
            if z_tok[c] is not None and psB[c].ready is None:
                psB[c].wrote(z_tok[ZCH - 1])
        wb.read(last)
    t_Z = []
    for c in range(ZCH):
        t = pr.tick("dve", lambda en, c=c: en.tensor_copy(
            out=ZS[:, c * 512:(c + 1) * 512], in_=psS[c][0:NH, 0:512]),
            deps=[z_tok[ZCH - 1]])
        psB[c].read(t)
        t_Z.append(t)
    t_ZT = []
    for kt in range(KT):
        d4 = psB[4].begin_write()
        tt_ = pr.tick("pe", lambda en, kt=kt: en.transpose(
            out=psS[4][:, 0:NH], in_=ZS[0:NH, kt * P:(kt + 1) * P],
            identity=identf[0:NH, 0:NH]),
            deps=t_Z + [t_id] + d4)
        psB[4].wrote(tt_)
        te = pr.tick("dve", lambda en, kt=kt: en.tensor_copy(
            out=ZTf[:, kt * NH:(kt + 1) * NH], in_=psS[4][:, 0:NH]),
            deps=[tt_])
        psB[4].read(te)
        t_ZT.append(te)
    t_ZTb = pr.tick("dve", lambda en: en.tensor_copy(
        out=ZTb[:], in_=ZTf[:]), deps=t_ZT)

    ZS_ctx.__exit__(None, None, None)
    preWb_ctx.__exit__(None, None, None)
    preW_ctx.__exit__(None, None, None)
    cexp_ctx.__exit__(None, None, None)
    for b in psB:
        b.ready = None
        b.readers = []

    def ln_rows(stat_toks, bank_sum, bank_sq):
        rowdep = rowsB.begin_write()
        t_mu = []
        t_rs = []
        for c in range(NCW):
            tmu = pr.tick("dve", lambda en, c=c: en.tensor_scalar(
                out=muR[:, c * CW:(c + 1) * CW],
                in0=psS[bank_sum + c][0:1, 0:CW],
                scalar1=1.0 / D, scalar2=None, op0=ALU.mult),
                deps=list(stat_toks) + rowdep)
            psB[bank_sum + c].read(tmu)
            rowdep = []
            tms = pr.tick("dve", lambda en, c=c: en.tensor_scalar(
                out=msR[:, c * CW:(c + 1) * CW],
                in0=psS[bank_sq + c][0:1, 0:CW],
                scalar1=1.0 / D, scalar2=None, op0=ALU.mult),
                deps=stat_toks)
            psB[bank_sq + c].read(tms)
            tv1 = pr.tick("dve", lambda en, c=c: en.tensor_tensor(
                out=rstdR[:, c * CW:(c + 1) * CW],
                in0=muR[:, c * CW:(c + 1) * CW],
                in1=muR[:, c * CW:(c + 1) * CW], op=ALU.mult), deps=[tmu])
            tv2 = pr.tick("dve", lambda en, c=c: en.tensor_tensor(
                out=msR[:, c * CW:(c + 1) * CW],
                in0=msR[:, c * CW:(c + 1) * CW],
                in1=rstdR[:, c * CW:(c + 1) * CW], op=ALU.subtract),
                deps=[tms, tv1])
            tve = pr.tick("dve", lambda en, c=c: en.tensor_scalar(
                out=msR[:, c * CW:(c + 1) * CW],
                in0=msR[:, c * CW:(c + 1) * CW],
                scalar1=EPS, scalar2=None, op0=ALU.add), deps=[tv2])
            tsq = pr.tick("act", lambda en, c=c: en.activation(
                out=rstdR[:, c * CW:(c + 1) * CW],
                in_=msR[:, c * CW:(c + 1) * CW],
                func=AF.Sqrt), deps=[tve])
            trc = pr.tick("dve", lambda en, c=c: en.reciprocal(
                out=rstdR[:, c * CW:(c + 1) * CW],
                in_=rstdR[:, c * CW:(c + 1) * CW]), deps=[tsq])
            t_mu.append(tmu)
            t_rs.append(trc)
        rowsB.wrote(t_rs[-1])
        return t_mu, t_rs

    def bcast_rows(t_mu, t_rs, bank_a, bank_b):
        dmu = muB_b.begin_write()
        drs = rstdB_b.begin_write()
        toks = []
        for c in range(NCW):
            da = psB[bank_a].begin_write()
            tm = pr.tick("pe", lambda en, c=c, b=bank_a: en.matmul(
                out=psS[b][:, 0:CW], lhsT=onesrow_f[:],
                rhs=muR[:, c * CW:(c + 1) * CW], start=True, stop=True),
                deps=[t_mu[c], t_ones_f] + da + dmu)
            psB[bank_a].wrote(tm)
            rowsB.read(tm)
            tbm = pr.tick("dve", lambda en, c=c, b=bank_a: en.tensor_copy(
                out=muB[:, c * CW:(c + 1) * CW], in_=psS[b][:, 0:CW]),
                deps=[tm] + dmu)
            psB[bank_a].read(tbm)
            db = psB[bank_b].begin_write()
            tr = pr.tick("pe", lambda en, c=c, b=bank_b: en.matmul(
                out=psS[b][:, 0:CW], lhsT=onesrow_f[:],
                rhs=rstdR[:, c * CW:(c + 1) * CW], start=True, stop=True),
                deps=[t_rs[c], t_ones_f] + db + drs)
            psB[bank_b].wrote(tr)
            rowsB.read(tr)
            tbr = pr.tick("dve", lambda en, c=c, b=bank_b: en.tensor_copy(
                out=rstdB[:, c * CW:(c + 1) * CW], in_=psS[b][:, 0:CW]),
                deps=[tr] + drs)
            psB[bank_b].read(tbr)
            toks += [tbm, tbr]
            dmu = []
            drs = []
        muB_b.wrote(toks[-2])
        rstdB_b.wrote(toks[-1])
        return toks

    out_total = 0

    def emit_half(half, state_prev_half):
        nonlocal out_total
        hoff = half * TH

        with (
            nc.sbuf_tensor(f"yS{half}", [P, NH * TH], BF16) as yS,
            nc.sbuf_tensor(f"ygS{half}", [P, KT * TH], BF16) as ygS,
            nc.sbuf_tensor(f"vS{half}", [P, 3 * TH], F32) as vS,
            nc.sbuf_tensor(f"retS{half}", [P, 3 * TH], F32) as retS,
            nc.sbuf_tensor(f"sqS{half}", [P, 2 * CW], BF16) as sqS,
            nc.sbuf_tensor(f"cbS{half}", [P, 2 * CW], BF16) as cbS,
            nc.sbuf_tensor(f"ntmpS{half}", [P, 2 * CW], F32) as ntmpS,
            nc.sbuf_tensor(f"x1tS{half}", [P, 4 * CW], F32) as x1tS,
        ):
            xqS = xnS

            xn_dep = xn_b.begin_write()
            for kt in range(KT):
                t = pr.dma("pool", lambda en, kt=kt: en.dma_start(
                    out=xqS[:, kt * TH:(kt + 1) * TH],
                    in_=xfm_d[kt * P:(kt + 1) * P, hoff:hoff + TH]),
                    deps=xn_dep, sem="dX")
                xn_dep = []
            t_xq_all = ("dX", pr.counts["dX"])
            t_xq = [t_xq_all] * KT

            initS = carryS if half == 0 else stateS
            y_done = [None] * NH
            scan_done = [None] * NH
            state_save = [None] * NH

            for mt in range(NH):
                slot, wb, tld = load_wcol(wv_d[mt].rearrange("p kt m -> p (kt m)"))
                grp = None
                for kt in range(KT):
                    for c in range(NCW):
                        dep = [tld, t_xq[kt]]
                        if kt == 0:
                            dep += psB[c].begin_write()
                        grp = pr.step("pe", lambda en, kt=kt, c=c, slot=slot: en.matmul(
                            out=psS[c][:, 0:CW],
                            lhsT=wringS[slot][:, kt * P:(kt + 1) * P],
                            rhs=xqS[:, kt * TH + c * CW:kt * TH + (c + 1) * CW],
                            start=(kt == 0), stop=(kt == KT - 1)),
                            deps=dep,
                            inc="pe" if (kt == KT - 1 and c == NCW - 1 and half != 0) else None)
                if half == 0:
                    for kt in range(KT):
                        dep = [tld, t_ZTb]
                        if kt == 0:
                            dep += psB[4].begin_write()
                        grp = pr.step("pe", lambda en, kt=kt, mt=mt, slot=slot: en.matmul(
                            out=psS[4][:, mt:mt + 1],
                            lhsT=wringS[slot][:, kt * P:(kt + 1) * P],
                            rhs=ZTb[:, kt * NH + mt:kt * NH + mt + 1],
                            start=(kt == 0), stop=(kt == KT - 1)),
                            deps=dep,
                            inc="pe" if kt == KT - 1 else None)
                    psB[4].wrote(grp)
                for c in range(NCW):
                    psB[c].wrote(grp)
                wb.read(grp)
                vslot, vbuf = vR.next()
                vdep = vbuf.begin_write()
                ev = None
                for c in range(NCW):
                    ev = pr.tick("act", lambda en, vslot=vslot, c=c: en.copy(
                        out=vS[:, vslot * TH + c * CW:vslot * TH + (c + 1) * CW],
                        in_=psS[c][:, 0:CW]),
                        deps=[grp] + vdep)
                    psB[c].read(ev)
                    vdep = []
                vbuf.wrote(ev)
                if half == 0:
                    dd = carryB[mt].begin_write()
                    tce = pr.tick("dve", lambda en, mt=mt: en.tensor_copy(
                        out=carryS[:, mt:mt + 1], in_=psS[4][:, mt:mt + 1]),
                        deps=[grp] + dd)
                    psB[4].read(tce)
                    carryB[mt].wrote(tce)
                rslot, rbuf = retR.next()
                sdep = [ev, t_lamP] + rbuf.begin_write()
                if half == 0:
                    sdep.append(carryB[mt].ready)
                else:
                    sdep.append(state_prev_half[mt])
                scan_done[mt] = pr.tick("dve", lambda en, mt=mt, rslot=rslot, vslot=vslot: en.tensor_tensor_scan(
                    out=retS[:, rslot * TH:(rslot + 1) * TH],
                    data0=lamP[:, mt:mt + 1].broadcast_to([P, TH]),
                    data1=vS[:, vslot * TH:(vslot + 1) * TH],
                    initial=initS[:, mt:mt + 1],
                    op0=ALU.mult, op1=ALU.add), deps=sdep)
                vbuf.read(scan_done[mt])
                if half == 0:
                    carryB[mt].read(scan_done[mt])
                rbuf.wrote(scan_done[mt])
                sdd = stateB[mt].begin_write()
                state_save[mt] = pr.tick("dve", lambda en, mt=mt, rslot=rslot: en.tensor_copy(
                    out=stateS[:, mt:mt + 1],
                    in_=retS[:, rslot * TH + TH - 1:rslot * TH + TH]),
                    deps=[scan_done[mt]] + sdd)
                stateB[mt].wrote(state_save[mt])
                rbuf.read(state_save[mt])
                slot, wb, tld = load_wcol(wq_d[mt].rearrange("p kt m -> p (kt m)"))
                grp = None
                for kt in range(KT):
                    for c in range(NCW):
                        dep = [tld, t_xq[kt]]
                        if kt == 0:
                            dep += psB[2 + c].begin_write()
                        grp = pr.step("pe", lambda en, kt=kt, c=c, slot=slot: en.matmul(
                            out=psS[2 + c][:, 0:CW],
                            lhsT=wringS[slot][:, kt * P:(kt + 1) * P],
                            rhs=xqS[:, kt * TH + c * CW:kt * TH + (c + 1) * CW],
                            start=(kt == 0), stop=(kt == KT - 1)),
                            deps=dep,
                            inc="pe" if (kt == KT - 1 and c == NCW - 1) else None)
                for c in range(NCW):
                    psB[2 + c].wrote(grp)
                wb.read(grp)
                yd = None
                for c in range(NCW):
                    yd = pr.tick("dve", lambda en, mt=mt, c=c, rslot=rslot: en.tensor_tensor(
                        out=yS[:, mt * TH + c * CW:mt * TH + (c + 1) * CW],
                        in0=psS[2 + c][:, 0:CW],
                        in1=retS[:, rslot * TH + c * CW:rslot * TH + (c + 1) * CW],
                        op=ALU.mult),
                        deps=[grp, scan_done[mt]])
                    psB[2 + c].read(yd)
                y_done[mt] = yd
                rbuf.read(yd)

            state_next_half = state_save

            st_sum = [None] * NCW
            st_sq = [None] * NCW
            for mt in range(NH):
                for c in range(NCW):
                    sqslot, sqbuf = sqR.next()
                    tq = pr.tick("act", lambda en, mt=mt, c=c, sqslot=sqslot: en.activation(
                        out=sqS[:, sqslot * CW:(sqslot + 1) * CW],
                        in_=yS[:, mt * TH + c * CW:mt * TH + (c + 1) * CW],
                        func=AF.Square),
                        deps=[y_done[mt]] + sqbuf.begin_write())
                    sqbuf.wrote(tq)
                    dep0 = [y_done[mt], t_ones_bf]
                    if mt == 0:
                        dep0 += psB[c].begin_write()
                    st_sum[c] = pr.step("pe", lambda en, mt=mt, c=c: en.matmul(
                        out=psS[c][0:1, 0:CW], lhsT=onescol_bf[:],
                        rhs=yS[:, mt * TH + c * CW:mt * TH + (c + 1) * CW],
                        start=(mt == 0), stop=(mt == NH - 1)),
                        deps=dep0, inc="pe")
                    dep1 = [tq]
                    if mt == 0:
                        dep1 += psB[2 + c].begin_write()
                    st_sq[c] = pr.step("pe", lambda en, c=c, mt=mt, sqslot=sqslot: en.matmul(
                        out=psS[2 + c][0:1, 0:CW], lhsT=onescol_bf[:],
                        rhs=sqS[:, sqslot * CW:(sqslot + 1) * CW],
                        start=(mt == 0), stop=(mt == NH - 1)),
                        deps=dep1, inc="pe")
                    sqbuf.read(st_sq[c])
            for c in range(NCW):
                psB[c].wrote(st_sum[c])
                psB[2 + c].wrote(st_sq[c])

            t_mu1, t_rs1 = ln_rows(st_sum + st_sq, 0, 2)

            yg_sil = [None] * NH
            g_grp = [None] * NH
            for mt in range(NH):
                slot, wb, tld = load_wcol(wg_d[mt].rearrange("p kt m -> p (kt m)"))
                grp = None
                for kt in range(KT):
                    for c in range(NCW):
                        dep = [tld, t_xq[kt]]
                        if kt == 0:
                            dep += psB[c].begin_write()
                        grp = pr.step("pe", lambda en, kt=kt, c=c, slot=slot: en.matmul(
                            out=psS[c][:, 0:CW],
                            lhsT=wringS[slot][:, kt * P:(kt + 1) * P],
                            rhs=xqS[:, kt * TH + c * CW:kt * TH + (c + 1) * CW],
                            start=(kt == 0), stop=(kt == KT - 1)),
                            deps=dep,
                            inc="pe" if (kt == KT - 1 and c == NCW - 1) else None)
                for c in range(NCW):
                    psB[c].wrote(grp)
                wb.read(grp)
                g_grp[mt] = grp
                sil = None
                for c in range(NCW):
                    sil = pr.tick("act", lambda en, mt=mt, c=c: en.activation(
                        out=ygS[:, mt * TH + c * CW:mt * TH + (c + 1) * CW],
                        in_=psS[c][:, 0:CW], func=AF.Silu),
                        deps=[grp])
                    psB[c].read(sil)
                yg_sil[mt] = sil

            xn_b.read(g_grp[NH - 1])
            bt1 = bcast_rows(t_mu1, t_rs1, 4, 5)

            yg_done = [None] * NH
            for mt in range(NH):
                for c in range(NCW):
                    ntslot, ntbuf = ntR.next()
                    tn1 = pr.tick("dve", lambda en, mt=mt, c=c, ntslot=ntslot: en.tensor_tensor(
                        out=ntmpS[:, ntslot * CW:(ntslot + 1) * CW],
                        in0=yS[:, mt * TH + c * CW:mt * TH + (c + 1) * CW],
                        in1=muB[:, c * CW:(c + 1) * CW], op=ALU.subtract),
                        deps=[y_done[mt], muB_b.ready] + ntbuf.begin_write())
                    muB_b.read(tn1)
                    tn2 = pr.tick("dve", lambda en, mt=mt, c=c, ntslot=ntslot: en.scalar_tensor_tensor(
                        out=ntmpS[:, ntslot * CW:(ntslot + 1) * CW],
                        in0=ntmpS[:, ntslot * CW:(ntslot + 1) * CW],
                        scalar=lngamS[:, mt:mt + 1],
                        in1=rstdB[:, c * CW:(c + 1) * CW],
                        op0=ALU.mult, op1=ALU.mult),
                        deps=[tn1, t_lngam, rstdB_b.ready])
                    rstdB_b.read(tn2)
                    ygt = pr.tick("dve", lambda en, mt=mt, c=c, ntslot=ntslot: en.scalar_tensor_tensor(
                        out=ygS[:, mt * TH + c * CW:mt * TH + (c + 1) * CW],
                        in0=ntmpS[:, ntslot * CW:(ntslot + 1) * CW],
                        scalar=lnbetS[:, mt:mt + 1],
                        in1=ygS[:, mt * TH + c * CW:mt * TH + (c + 1) * CW],
                        op0=ALU.add, op1=ALU.mult),
                        deps=[tn2, yg_sil[mt], t_lnbet])
                    ntbuf.wrote(ygt)
                    ntbuf.read(ygt)
                    yg_done[mt] = ygt

            st2_sum = [None] * NCW
            st2_sq = [None] * NCW
            x1w_toks = {}
            for mt in range(NH):
                xfts = []
                for c in range(NCW):
                    xslot, xbuf = xfR.next()
                    t = pr.dma("sp", lambda en, mt=mt, c=c, xslot=xslot: en.dma_start(
                        out=xf32S[:, xslot * CW:(xslot + 1) * CW],
                        in_=xfm_d[mt * P:(mt + 1) * P, hoff + c * CW:hoff + (c + 1) * CW]),
                        deps=xbuf.begin_write(), sem=f"dR{xslot}")
                    xbuf.wrote(t)
                    xfts.append((xslot, xbuf, t))
                slot, wb, tld = load_wcol(wo_d[mt].rearrange("p kt m -> p (kt m)"))
                grp = None
                for kt in range(KT):
                    for c in range(NCW):
                        dep = [tld, yg_done[kt]]
                        if kt == 0:
                            dep += psB[2 + c].begin_write()
                        grp = pr.step("pe", lambda en, kt=kt, c=c, slot=slot: en.matmul(
                            out=psS[2 + c][:, 0:CW],
                            lhsT=wringS[slot][:, kt * P:(kt + 1) * P],
                            rhs=ygS[:, kt * TH + c * CW:kt * TH + (c + 1) * CW],
                            start=(kt == 0), stop=(kt == KT - 1)),
                            deps=dep,
                            inc="pe" if (kt == KT - 1 and c == NCW - 1) else None)
                for c in range(NCW):
                    psB[2 + c].wrote(grp)
                wb.read(grp)
                for c in range(NCW):
                    xslot, xbuf, xft = xfts[c]
                    x1slot, x1buf = x1R.next()
                    t_add = pr.tick("dve", lambda en, c=c, x1slot=x1slot, xslot=xslot: en.tensor_tensor(
                        out=x1tS[:, x1slot * CW:(x1slot + 1) * CW],
                        in0=psS[2 + c][:, 0:CW],
                        in1=xf32S[:, xslot * CW:(xslot + 1) * CW],
                        op=ALU.add),
                        deps=[grp, xft] + x1buf.begin_write())
                    psB[2 + c].read(t_add)
                    xbuf.read(t_add)
                    x1buf.wrote(t_add)
                    tw = pr.dma("sp", lambda en, mt=mt, c=c, x1slot=x1slot: en.dma_start(
                        out=x1hbm[mt * P:(mt + 1) * P, hoff + c * CW:hoff + (c + 1) * CW],
                        in_=x1tS[:, x1slot * CW:(x1slot + 1) * CW]),
                        deps=[t_add], sem=f"dW{x1slot}")
                    x1buf.read(tw)
                    x1w_toks[(mt, c)] = tw
                    cbslot, cbbuf = cbR.next()
                    tc_ = pr.tick("dve", lambda en, x1slot=x1slot, cbslot=cbslot: en.tensor_copy(
                        out=cbS[:, cbslot * CW:(cbslot + 1) * CW],
                        in_=x1tS[:, x1slot * CW:(x1slot + 1) * CW]),
                        deps=[t_add] + cbbuf.begin_write())
                    cbbuf.wrote(tc_)
                    x1buf.read(tc_)
                    sqslot, sqbuf = sqR.next()
                    tq_ = pr.tick("act", lambda en, x1slot=x1slot, sqslot=sqslot: en.activation(
                        out=sqS[:, sqslot * CW:(sqslot + 1) * CW],
                        in_=x1tS[:, x1slot * CW:(x1slot + 1) * CW],
                        func=AF.Square),
                        deps=[t_add] + sqbuf.begin_write())
                    sqbuf.wrote(tq_)
                    x1buf.read(tq_)
                    dep0 = [tc_, t_ones_bf]
                    if mt == 0:
                        dep0 += psB[4 + c].begin_write()
                    st2_sum[c] = pr.step("pe", lambda en, mt=mt, c=c, cbslot=cbslot: en.matmul(
                        out=psS[4 + c][0:1, 0:CW], lhsT=onescol_bf[:],
                        rhs=cbS[:, cbslot * CW:(cbslot + 1) * CW],
                        start=(mt == 0), stop=(mt == NH - 1)),
                        deps=dep0, inc="pe")
                    cbbuf.read(st2_sum[c])
                    dep1 = [tq_]
                    if mt == 0:
                        dep1 += psB[6 + c].begin_write()
                    st2_sq[c] = pr.step("pe", lambda en, mt=mt, c=c, sqslot=sqslot: en.matmul(
                        out=psS[6 + c][0:1, 0:CW], lhsT=onescol_bf[:],
                        rhs=sqS[:, sqslot * CW:(sqslot + 1) * CW],
                        start=(mt == 0), stop=(mt == NH - 1)),
                        deps=dep1, inc="pe")
                    sqbuf.read(st2_sq[c])
            for c in range(NCW):
                psB[4 + c].wrote(st2_sum[c])
                psB[6 + c].wrote(st2_sq[c])
            t_x1w_all = [(f"dW{k}", pr.counts[f"dW{k}"]) for k in range(4)]

            t_mu2, t_rs2 = ln_rows(st2_sum + st2_sq, 4, 6)
            bt2 = bcast_rows(t_mu2, t_rs2, 6, 7)

            xn_war = g_grp[NH - 1]
            x1n_tok = {}
            for kt in range(KT):
                eng = "dve" if kt % 2 == 0 else "pool"
                for c in range(NCW):
                    xslot, xbuf = xfR.next()
                    trb = pr.dma("sp", lambda en, kt=kt, c=c, xslot=xslot: en.dma_start(
                        out=xf32S[:, xslot * CW:(xslot + 1) * CW],
                        in_=x1hbm[kt * P:(kt + 1) * P, hoff + c * CW:hoff + (c + 1) * CW]),
                        deps=t_x1w_all + xbuf.begin_write(), sem=f"dR{xslot}")
                    xbuf.wrote(trb)
                    ntslot, ntbuf = ntR.next()
                    tn1 = pr.tick(eng, lambda en, xslot=xslot, c=c, ntslot=ntslot: en.tensor_tensor(
                        out=ntmpS[:, ntslot * CW:(ntslot + 1) * CW],
                        in0=xf32S[:, xslot * CW:(xslot + 1) * CW],
                        in1=muB[:, c * CW:(c + 1) * CW], op=ALU.subtract),
                        deps=[trb, muB_b.ready] + ntbuf.begin_write())
                    muB_b.read(tn1)
                    xbuf.read(tn1)
                    tn2 = pr.tick(eng, lambda en, kt=kt, c=c, ntslot=ntslot: en.tensor_tensor(
                        out=xnS[:, kt * TH + c * CW:kt * TH + (c + 1) * CW],
                        in0=ntmpS[:, ntslot * CW:(ntslot + 1) * CW],
                        in1=rstdB[:, c * CW:(c + 1) * CW],
                        op=ALU.mult),
                        deps=[tn1, rstdB_b.ready, xn_war])
                    rstdB_b.read(tn2)
                    ntbuf.wrote(tn1)
                    ntbuf.read(tn2)
                    x1n_tok[(kt, c)] = tn2
                    xn_b.wrote(tn2)
            t_x1nw_all = [(f"dA{k}", pr.counts[f"dA{k}"]) for k in range(2)]

            pr.barrier()
            for b in psB:
                b.ready = None
                b.readers = []

        with (
            nc.sbuf_tensor(f"gbufS{half}", [P, 2 * FT * TH], BF16) as gbufS,
            nc.sbuf_tensor(f"x2S{half}", [P, KT * TH], F32) as x2S,
            nc.sbuf_tensor(f"sh1S{half}", [P, 2 * CW], BF16) as sh1S,
            nc.sbuf_tensor(f"gtmpS{half}", [P, 2 * CW], BF16) as gtmpS,
            nc.sbuf_tensor(f"ot2S{half}", [P, 4 * CW], F32) as ot2S,
        ):
            xrbS = xf32S
            xrR = xfR

            x1nS = xnS
            t_x1n = [x1n_tok[(kt, NCW - 1)] if (kt, NCW - 1) in x1n_tok
                     else x1n_tok[(kt, 0)] for kt in range(KT)]
            t_x1n = [None] * KT
            for kt in range(KT):
                toks = [x1n_tok[(kt, c)] for c in range(NCW)]
                t_x1n[kt] = max(toks, key=lambda t: t[1])

            rl_grp = [None] * NCW
            for kt in range(KT):
                for c in range(NCW):
                    dep = [t_wgf, t_x1n[kt]]
                    if kt == 0:
                        dep += psB[c].begin_write()
                    rl_grp[c] = pr.step("pe", lambda en, kt=kt, c=c: en.matmul(
                        out=psS[c][0:E, 0:CW],
                        lhsT=wgfS[:, kt * E:(kt + 1) * E],
                        rhs=x1nS[:, kt * TH + c * CW:kt * TH + (c + 1) * CW],
                        start=(kt == 0), stop=(kt == KT - 1)),
                        deps=dep, inc="pe" if kt == KT - 1 else None)
            for c in range(NCW):
                psB[c].wrote(rl_grp[c])
            t_rl = []
            for c in range(NCW):
                t = pr.tick("act", lambda en, c=c: en.activation(
                    out=r8a[:, c * CW:(c + 1) * CW], in_=psS[c][0:E, 0:CW],
                    func=AF.Sigmoid), deps=[rl_grp[c]])
                psB[c].read(t)
                t_rl.append(t)
            t_sc = pr.tick("dve", lambda en: en.tensor_scalar(
                out=r8a[:, 0:TH], in0=r8a[:, 0:TH], scalar1=fbiasS[:, 0:1],
                scalar2=None, op0=ALU.add), deps=t_rl + [t_fbias])
            ew_done = []
            for tt in range(TT):
                d2 = psB[2].begin_write()
                ttr = pr.tick("pe", lambda en, tt=tt: en.transpose(
                    out=psS[2][:, 0:E], in_=r8a[0:E, tt * P:(tt + 1) * P],
                    identity=identf[0:E, 0:E]),
                    deps=[t_sc, t_id] + d2)
                psB[2].wrote(ttr)
                tev = pr.tick("dve", lambda en: en.tensor_copy(
                    out=sctS[:, 0:E], in_=psS[2][:, 0:E]),
                    deps=[ttr] + ew_done[-1:])
                psB[2].read(tev)
                trw = pr.tick("dve", lambda en: en.tensor_tensor(
                    out=ewtS[:, 0:E], in0=sctS[:, 0:E], in1=fbB[:, 0:E],
                    op=ALU.subtract), deps=[tev, t_fbB])
                tm8 = pr.tick("dve", lambda en: en.max(
                    out=rt_m8[:], in_=sctS[:, 0:E]), deps=[tev])
                ti8 = pr.tick("dve", lambda en: en.max_index(
                    out=rt_i8[:], in_max=rt_m8[:], in_values=sctS[:, 0:E]),
                    deps=[tm8])
                tif = pr.tick("dve", lambda en: en.tensor_copy(
                    out=rt_i8f[:], in_=rt_i8[:]), deps=[ti8])
                to1 = pr.tick("dve", lambda en: en.tensor_scalar(
                    out=rt_oh1[:], in0=iota8f[:], scalar1=rt_i8f[:, 0:1],
                    scalar2=None, op0=ALU.is_equal), deps=[tif, t_io8])
                to2 = pr.tick("dve", lambda en: en.tensor_scalar(
                    out=rt_oh2[:], in0=iota8f[:], scalar1=rt_i8f[:, 1:2],
                    scalar2=None, op0=ALU.is_equal), deps=[tif, t_io8])
                tw1 = pr.tick("dve", lambda en: en.scalar_tensor_tensor(
                    out=rt_jnk[:], in0=rt_oh1[:], scalar=1.0,
                    in1=ewtS[:, 0:E], op0=ALU.mult, op1=ALU.mult,
                    accum_out=rt_w1[:]), deps=[to1, trw])
                tw2 = pr.tick("dve", lambda en: en.scalar_tensor_tensor(
                    out=rt_jnk[:], in0=rt_oh2[:], scalar=1.0,
                    in1=ewtS[:, 0:E], op0=ALU.mult, op1=ALU.mult,
                    accum_out=rt_w2[:]), deps=[to2, trw, tw1])
                tws = pr.tick("dve", lambda en: en.tensor_tensor(
                    out=rt_ws[:], in0=rt_w1[:], in1=rt_w2[:], op=ALU.add),
                    deps=[tw1, tw2])
                tri = pr.tick("dve", lambda en: en.reciprocal(
                    out=rt_ws[:], in_=rt_ws[:]), deps=[tws])
                tn1 = pr.tick("dve", lambda en: en.tensor_tensor(
                    out=rt_w1[:], in0=rt_w1[:], in1=rt_ws[:], op=ALU.mult),
                    deps=[tri])
                tn2 = pr.tick("dve", lambda en: en.tensor_tensor(
                    out=rt_w2[:], in0=rt_w2[:], in1=rt_ws[:], op=ALU.mult),
                    deps=[tri, tn1])
                te1 = pr.tick("dve", lambda en: en.tensor_scalar(
                    out=ewtS[:, 0:E], in0=rt_oh1[:],
                    scalar1=rt_w1[:, 0:1], scalar2=None, op0=ALU.mult),
                    deps=[tn1, trw])
                te2 = pr.tick("dve", lambda en: en.scalar_tensor_tensor(
                    out=ewtS[:, 0:E], in0=rt_oh2[:], scalar=rt_w2[:, 0:1],
                    in1=ewtS[:, 0:E], op0=ALU.mult, op1=ALU.add),
                    deps=[tn2, te1])
                d3 = psB[3].begin_write()
                ttb = pr.tick("pe", lambda en: en.transpose(
                    out=psS[3][0:E, 0:P], in_=ewtS[:, 0:E],
                    identity=identf[:, :]), deps=[te2, t_id] + d3)
                psB[3].wrote(ttb)
                tee = pr.tick("dve", lambda en, tt=tt: en.tensor_copy(
                    out=ewTf[0:E, tt * P:(tt + 1) * P], in_=psS[3][0:E, 0:P]),
                    deps=[ttb])
                psB[3].read(tee)
                ew_done.append(tee)

            for pp in range(4):
                g_done = {}
                for ei in range(2):
                    e = 2 * pp + ei
                    ewb_done = []
                    dw = ewbB.begin_write()
                    for c in range(NCW):
                        d6 = psB[6].begin_write()
                        tb = pr.tick("pe", lambda en, e=e, c=c: en.matmul(
                            out=psS[6][:, 0:CW],
                            lhsT=e8f[:, e * P:(e + 1) * P],
                            rhs=ewTf[0:E, c * CW:(c + 1) * CW],
                            start=True, stop=True),
                            deps=ew_done + [t_e8] + d6)
                        psB[6].wrote(tb)
                        te = pr.tick("dve", lambda en, c=c: en.tensor_copy(
                            out=ewbS[:, c * CW:(c + 1) * CW],
                            in_=psS[6][:, 0:CW]), deps=[tb] + dw)
                        psB[6].read(te)
                        ewb_done.append(te)
                        dw = []
                    ewbB.wrote(ewb_done[-1])
                    for ft in range(FT):
                        slot1, wb1, tld1 = load_wcol(
                            w1_d[e, ft].rearrange("p kt m -> p (kt m)"))
                        g1 = None
                        for kt in range(KT):
                            for c in range(NCW):
                                dep = [tld1, t_x1n[kt]]
                                if kt == 0:
                                    dep += psB[c].begin_write()
                                g1 = pr.step("pe", lambda en, kt=kt, c=c, slot=slot1: en.matmul(
                                    out=psS[c][:, 0:CW],
                                    lhsT=wringS[slot][:, kt * P:(kt + 1) * P],
                                    rhs=x1nS[:, kt * TH + c * CW:kt * TH + (c + 1) * CW],
                                    start=(kt == 0), stop=(kt == KT - 1)),
                                    deps=dep,
                                    inc="pe" if (kt == KT - 1 and c == NCW - 1) else None)
                        for c in range(NCW):
                            psB[c].wrote(g1)
                        wb1.read(g1)
                        slot3, wb3, tld3 = load_wcol(
                            w3_d[e, ft].rearrange("p kt m -> p (kt m)"))
                        g3 = None
                        for kt in range(KT):
                            for c in range(NCW):
                                dep = [tld3, t_x1n[kt]]
                                if kt == 0:
                                    dep += psB[2 + c].begin_write()
                                g3 = pr.step("pe", lambda en, kt=kt, c=c, slot=slot3: en.matmul(
                                    out=psS[2 + c][:, 0:CW],
                                    lhsT=wringS[slot][:, kt * P:(kt + 1) * P],
                                    rhs=x1nS[:, kt * TH + c * CW:kt * TH + (c + 1) * CW],
                                    start=(kt == 0), stop=(kt == KT - 1)),
                                    deps=dep,
                                    inc="pe" if (kt == KT - 1 and c == NCW - 1) else None)
                        for c in range(NCW):
                            psB[2 + c].wrote(g3)
                        wb3.read(g3)
                        ffn_last_g3 = g3
                        gbb = gbufB[ei][ft]
                        gdp = gbb.begin_write()
                        gd = None
                        for c in range(NCW):
                            shslot, shbuf = sh1R.next()
                            ts1 = pr.tick("act", lambda en, c=c, shslot=shslot: en.activation(
                                out=sh1S[:, shslot * CW:(shslot + 1) * CW],
                                in_=psS[c][:, 0:CW], func=AF.Silu),
                                deps=[g1] + shbuf.begin_write())
                            psB[c].read(ts1)
                            shbuf.wrote(ts1)
                            gtslot, gtbuf = gtR.next()
                            tg1 = pr.tick("dve", lambda en, c=c, shslot=shslot, gtslot=gtslot: en.tensor_tensor(
                                out=gtmpS[:, gtslot * CW:(gtslot + 1) * CW],
                                in0=psS[2 + c][:, 0:CW],
                                in1=sh1S[:, shslot * CW:(shslot + 1) * CW],
                                op=ALU.mult),
                                deps=[g3, ts1] + gtbuf.begin_write())
                            psB[2 + c].read(tg1)
                            shbuf.read(tg1)
                            gtbuf.wrote(tg1)
                            gd = pr.tick("dve", lambda en, ei=ei, ft=ft, c=c, gtslot=gtslot: en.tensor_tensor(
                                out=gbufS[:, (ei * FT + ft) * TH + c * CW:(ei * FT + ft) * TH + (c + 1) * CW],
                                in0=ewbS[:, c * CW:(c + 1) * CW],
                                in1=gtmpS[:, gtslot * CW:(gtslot + 1) * CW],
                                op=ALU.mult),
                                deps=[tg1, ewb_done[c]] + gdp)
                            ewbB.read(gd)
                            gtbuf.read(gd)
                            gdp = []
                        gbb.wrote(gd)
                        g_done[(ei, ft)] = gd
                for ht in range(KT):
                    slot2, wb2, tld2 = load_wcol(
                        w2_d[ht][:, 2 * pp:2 * pp + 2].rearrange("p e f m -> p (e f m)"))
                    grp_last = None
                    for c in range(NCW):
                        grp = None
                        for k in range(2 * FT):
                            ei, ft = divmod(k, FT)
                            dep = [tld2, g_done[(ei, ft)]]
                            if k == 0:
                                dep += psB[4 + c].begin_write()
                            grp = pr.step("pe", lambda en, k=k, c=c, slot=slot2: en.matmul(
                                out=psS[4 + c][:, 0:CW],
                                lhsT=wringS[slot][:, k * P:(k + 1) * P],
                                rhs=gbufS[:, k * TH + c * CW:k * TH + (c + 1) * CW],
                                start=(k == 0), stop=(k == 2 * FT - 1)),
                                deps=dep,
                                inc="pe" if k == 2 * FT - 1 else None)
                        psB[4 + c].wrote(grp)
                        grp_last = grp
                        x2b = x2B[ht][c]
                        if pp == 0:
                            xrslot, xrbuf = xrR.next()
                            trb = pr.dma("sp", lambda en, ht=ht, c=c, xrslot=xrslot: en.dma_start(
                                out=xrbS[:, xrslot * CW:(xrslot + 1) * CW],
                                in_=x1hbm[ht * P:(ht + 1) * P, hoff + c * CW:hoff + (c + 1) * CW]),
                                deps=t_x1w_all + xrbuf.begin_write(), sem=f"dR{xrslot}")
                            xrbuf.wrote(trb)
                            tx2 = pr.tick("dve", lambda en, ht=ht, c=c, xrslot=xrslot: en.tensor_tensor(
                                out=x2S[:, ht * TH + c * CW:ht * TH + (c + 1) * CW],
                                in0=psS[4 + c][:, 0:CW],
                                in1=xrbS[:, xrslot * CW:(xrslot + 1) * CW],
                                op=ALU.add),
                                deps=[grp, trb] + x2b.begin_write())
                            xrbuf.read(tx2)
                        else:
                            tx2 = pr.tick("dve", lambda en, ht=ht, c=c: en.scalar_tensor_tensor(
                                out=x2S[:, ht * TH + c * CW:ht * TH + (c + 1) * CW],
                                in0=psS[4 + c][:, 0:CW], scalar=1.0,
                                in1=x2S[:, ht * TH + c * CW:ht * TH + (c + 1) * CW],
                                op0=ALU.mult, op1=ALU.add),
                                deps=[grp, x2b.ready])
                        psB[4 + c].read(tx2)
                        x2b.wrote(tx2)
                    wb2.read(grp_last)
                for ei in range(2):
                    for ft in range(FT):
                        gbufB[ei][ft].read(grp_last)

            xn_b.read(ffn_last_g3)

            st3_sum = [None] * NCW
            st3_sq = [None] * NCW
            for ht in range(KT):
                for c in range(NCW):
                    x2b = x2B[ht][c]
                    cbslot, cbbuf = gtR.next()
                    tc_ = pr.tick("dve", lambda en, ht=ht, c=c, cbslot=cbslot: en.tensor_copy(
                        out=gtmpS[:, cbslot * CW:(cbslot + 1) * CW],
                        in_=x2S[:, ht * TH + c * CW:ht * TH + (c + 1) * CW]),
                        deps=[x2b.ready] + cbbuf.begin_write())
                    cbbuf.wrote(tc_)
                    x2b.read(tc_)
                    sqslot, sqbuf = sh1R.next()
                    tq_ = pr.tick("act", lambda en, ht=ht, c=c, sqslot=sqslot: en.activation(
                        out=sh1S[:, sqslot * CW:(sqslot + 1) * CW],
                        in_=x2S[:, ht * TH + c * CW:ht * TH + (c + 1) * CW],
                        func=AF.Square),
                        deps=[x2b.ready] + sqbuf.begin_write())
                    sqbuf.wrote(tq_)
                    x2b.read(tq_)
                    dep0 = [tc_, t_ones_bf]
                    if ht == 0:
                        dep0 += psB[4 + c].begin_write()
                    st3_sum[c] = pr.step("pe", lambda en, ht=ht, c=c, cbslot=cbslot: en.matmul(
                        out=psS[4 + c][0:1, 0:CW], lhsT=onescol_bf[:],
                        rhs=gtmpS[:, cbslot * CW:(cbslot + 1) * CW],
                        start=(ht == 0), stop=(ht == KT - 1)),
                        deps=dep0, inc="pe")
                    cbbuf.read(st3_sum[c])
                    dep1 = [tq_]
                    if ht == 0:
                        dep1 += psB[6 + c].begin_write()
                    st3_sq[c] = pr.step("pe", lambda en, ht=ht, c=c, sqslot=sqslot: en.matmul(
                        out=psS[6 + c][0:1, 0:CW], lhsT=onescol_bf[:],
                        rhs=sh1S[:, sqslot * CW:(sqslot + 1) * CW],
                        start=(ht == 0), stop=(ht == KT - 1)),
                        deps=dep1, inc="pe")
                    sqbuf.read(st3_sq[c])
            for c in range(NCW):
                psB[4 + c].wrote(st3_sum[c])
                psB[6 + c].wrote(st3_sq[c])

            t_mu3, t_rs3 = ln_rows(st3_sum + st3_sq, 4, 6)
            bt3 = bcast_rows(t_mu3, t_rs3, 4, 5)

            for ht in range(KT):
                eng3 = "dve" if ht % 2 == 0 else "pool"
                for c in range(NCW):
                    x2b = x2B[ht][c]
                    otslot, otbuf = otR.next()
                    tn1 = pr.tick(eng3, lambda en, ht=ht, c=c, otslot=otslot: en.tensor_tensor(
                        out=ot2S[:, otslot * CW:(otslot + 1) * CW],
                        in0=x2S[:, ht * TH + c * CW:ht * TH + (c + 1) * CW],
                        in1=muB[:, c * CW:(c + 1) * CW], op=ALU.subtract),
                        deps=[x2b.ready, muB_b.ready] + otbuf.begin_write())
                    muB_b.read(tn1)
                    x2b.read(tn1)
                    tn2 = pr.tick(eng3, lambda en, otslot=otslot, c=c: en.tensor_tensor(
                        out=ot2S[:, otslot * CW:(otslot + 1) * CW],
                        in0=ot2S[:, otslot * CW:(otslot + 1) * CW],
                        in1=rstdB[:, c * CW:(c + 1) * CW], op=ALU.mult),
                        deps=[tn1, rstdB_b.ready])
                    rstdB_b.read(tn2)
                    otbuf.wrote(tn2)
                    tw = pr.dma("act", lambda en, ht=ht, c=c, otslot=otslot: en.dma_start(
                        out=out_d[ht * P:(ht + 1) * P, hoff + c * CW:hoff + (c + 1) * CW],
                        in_=ot2S[:, otslot * CW:(otslot + 1) * CW]),
                        deps=[tn2], sem=f"out{otslot}")
                    otbuf.read(tw)
                    out_total += 16

            for k in range(4):
                tokk = (f"out{k}", pr.counts[f"out{k}"])
                for b in x1R.bufs:
                    b.read(tokk)
        return state_next_half

    st_half = [None] * NH
    for half in range(2):
        st_half = emit_half(half, st_half)

    dma_sems = ["dS", "dN", "dX"] + \
        [f"dG{i}" for i in range(4)] + [f"dR{i}" for i in range(4)] + \
        [f"dW{i}" for i in range(4)] + [f"dA{i}" for i in range(2)] + \
        [f"out{i}" for i in range(4)]
    final_deps = [(sname, pr.counts[sname]) for sname in dma_sems]
    out_deps = [(f"out{i}", pr.counts[f"out{i}"]) for i in range(4)]
    pr.step("act", lambda en: en.nop(), deps=final_deps)
    pr.step("sp", lambda en: en.nop(), deps=out_deps)

    pr.emit()
    return nc


_NCORES = 8


def _tile_attn_w(w):
    a = np.ascontiguousarray(w.T).reshape(KT, P, KT, P)
    return np.ascontiguousarray(a.transpose(2, 1, 0, 3))


def _tile_w13(w):
    a = np.ascontiguousarray(w.T).reshape(KT, P, FT, P)
    return np.ascontiguousarray(a.transpose(2, 1, 0, 3))


def _tile_w2(w):
    a = np.ascontiguousarray(w.T).reshape(FT, P, KT, P)
    return np.ascontiguousarray(a.transpose(2, 1, 0, 3))


def prep_inputs(L0_attn_x, L0_attn_Wq, L0_attn_Wv, L0_attn_Wo, L0_attn_Wg,
                L0_attn_ln_gamma, L0_attn_ln_beta, L0_attn_beta,
                L0_ffn_Wg, L0_ffn_W1, L0_ffn_W2, L0_ffn_W3, L0_ffn_bias,
                TC=2048, ncores=_NCORES):
    x = np.asarray(L0_attn_x, dtype=np.float32)
    B, L, _ = x.shape
    CPB = L // TC
    assert B * CPB == ncores
    PREF = (CPB - 1) * TC
    PT = PREF // P

    wq = _tile_attn_w(np.asarray(L0_attn_Wq, np.float32))
    wv = _tile_attn_w(np.asarray(L0_attn_Wv, np.float32))
    wg = _tile_attn_w(np.asarray(L0_attn_Wg, np.float32))
    wo = _tile_attn_w(np.asarray(L0_attn_Wo, np.float32))
    w1 = np.stack([_tile_w13(np.asarray(L0_ffn_W1[e], np.float32))
                   for e in range(E)])
    w3 = np.stack([_tile_w13(np.asarray(L0_ffn_W3[e], np.float32))
                   for e in range(E)])
    w2e = np.stack([_tile_w2(np.asarray(L0_ffn_W2[e], np.float32))
                    for e in range(E)])
    w2 = np.ascontiguousarray(w2e.transpose(1, 2, 0, 3, 4))
    wgf = np.ascontiguousarray(
        np.asarray(L0_ffn_Wg, np.float32).T.reshape(KT, P, E).transpose(1, 0, 2))
    lngam = np.ascontiguousarray(
        np.asarray(L0_attn_ln_gamma, np.float32).reshape(KT, P).T)
    lnbet = np.ascontiguousarray(
        np.asarray(L0_attn_ln_beta, np.float32).reshape(KT, P).T)
    hbeta = np.ascontiguousarray(np.asarray(L0_attn_beta, np.float32).reshape(NH, 1))
    fbias = np.ascontiguousarray(np.asarray(L0_ffn_bias, np.float32).reshape(E, 1))
    cexp = np.ascontiguousarray(
        (PREF - 1 - np.arange(PREF, dtype=np.float32)).reshape(PT, P).T)

    shared = dict(wqT=wq, wvT=wv, wgT=wg, woT=wo, w1T=w1, w3T=w3, w2T=w2,
                  wgfT=wgf, lngam=lngam, lnbet=lnbet, hbeta=hbeta,
                  fbias=fbias, cexp=cexp)

    in_maps = []
    for core in range(ncores):
        b = core // CPB
        t0 = (core % CPB) * TC
        m = dict(shared)
        m["xfm"] = np.ascontiguousarray(x[b, t0:t0 + TC, :].T)
        xpre = np.zeros((PREF, D), np.float32)
        if t0 > 0:
            xpre[PREF - t0:] = x[b, 0:t0]
        m["xpre"] = xpre
        in_maps.append(m)
    return in_maps, B, L, CPB, PREF


def kernel(**inputs):
    TC = 2048
    in_maps, B, L, CPB, PREF = prep_inputs(**inputs, TC=TC)
    nc = build(TC=TC, PREF=PREF)
    res = run_bass_kernel_spmd(nc, in_maps, core_ids=list(range(_NCORES)))
    out = np.empty((B, L, D), np.float32)
    for core in range(_NCORES):
        b = core // CPB
        t0 = (core % CPB) * TC
        out[b, t0:t0 + TC, :] = np.asarray(res.results[core]["outfm"]).T
    return out


# version 07
# speedup vs baseline: 1.0539x; 1.0539x over previous
import numpy as np
import ml_dtypes
import concourse.bass as bass
import concourse.mybir as mybir
from concourse.bass_utils import run_bass_kernel_spmd

F32 = mybir.dt.float32
BF16 = mybir.dt.bfloat16
U32 = mybir.dt.uint32
AF = mybir.ActivationFunctionType
ALU = mybir.AluOpType

P = 128
D = 2048
KT = D // P
NH = 16
FF = 1024
FT = FF // P
E = 8
NEGLN2 = -0.6931471805599453
EPS = 1e-5
TINY = 1.1754943508222875e-38

ENGINES = ("pe", "dve", "act", "pool", "sp")


class Prog:

    def __init__(self, nc):
        self.nc = nc
        self.sems = {}
        self.counts = {}
        self.steps = {e: [] for e in ENGINES}
        names = ["dS", "dN", "dX", "pe", "dve", "act", "pool", "sp"]
        names += [f"dG{i}" for i in range(4)] + [f"dR{i}" for i in range(4)]
        names += [f"dW{i}" for i in range(4)] + [f"dA{i}" for i in range(2)]
        names += [f"dC{i}" for i in range(4)]
        names += [f"out{i}" for i in range(4)]
        for s in names:
            self.sems[s] = nc.alloc_semaphore(s)
            self.counts[s] = 0

    def step(self, eng, action, deps=(), inc=None, inc_by=1):
        tok = None
        if inc is not None:
            self.counts[inc] += inc_by
            tok = (inc, self.counts[inc])
        self.steps[eng].append(
            (tuple(d for d in deps if d is not None), action, inc, inc_by))
        return tok

    def tick(self, eng, action, deps=()):
        return self.step(eng, action, deps, inc=eng)

    def dma(self, queue, action, deps=(), sem="dS"):
        return self.step(queue, action, deps, inc=sem, inc_by=16)

    def barrier(self):
        toks = [(e, self.counts[e]) for e in ENGINES]
        dma_toks = [(sname, self.counts[sname]) for sname in self.sems
                    if sname not in ENGINES]
        for e in ENGINES:
            deps = [(e, self.counts[e])] + \
                [t for t in toks if t[0] != e] + dma_toks
            self.tick(e, lambda en: en.nop(), deps)

    def emit(self):
        nc = self.nc

        def runner(eng):
            def run(engobj):
                lastwait = {}
                for deps, action, inc, inc_by in self.steps[eng]:
                    for s, v in deps:
                        if lastwait.get(s, 0) < v:
                            engobj.wait_ge(self.sems[s], v)
                            lastwait[s] = v
                    ins = action(engobj)
                    if inc is not None:
                        ins.then_inc(self.sems[inc], inc_by)
            return run

        with nc.Block() as block:
            block.sync(runner("sp"))
            block.gpsimd(runner("pool"))
            block.scalar(runner("act"))
            block.vector(runner("dve"))
            block.tensor(runner("pe"))


class Buf:

    def __init__(self):
        self.ready = None
        self.readers = []

    def begin_write(self):
        d = self.readers + ([self.ready] if self.ready else [])
        self.readers = []
        return d

    def wrote(self, tok):
        self.ready = tok

    def read(self, tok):
        self.readers.append(tok)
        return self.ready


class Ring:

    def __init__(self, n):
        self.bufs = [Buf() for _ in range(n)]
        self.i = 0

    def next(self):
        s = self.i % len(self.bufs)
        self.i += 1
        return s, self.bufs[s]


def build(TC=2048, PREF=6144, CW=512):
    TH = TC // 2
    CW = min(CW, TH)
    NCW = TH // CW
    assert NCW in (1, 2)
    PT = PREF // P
    TT = TH // P

    nc = bass.Bass("TRN2", target_bir_lowering=False)
    pr = Prog(nc)

    dt = nc.dram_tensor
    xfm_d = dt("xfm", [D, TC], F32, kind="ExternalInput").ap()
    xpre_d = dt("xpre", [PREF, D], F32, kind="ExternalInput").ap()
    wq_d = dt("wqT", [KT, P, KT, P], F32, kind="ExternalInput").ap()
    wv_d = dt("wvT", [KT, P, KT, P], F32, kind="ExternalInput").ap()
    wg_d = dt("wgT", [KT, P, KT, P], F32, kind="ExternalInput").ap()
    wo_d = dt("woT", [KT, P, KT, P], F32, kind="ExternalInput").ap()
    w1_d = dt("w1T", [E, FT, P, KT, P], F32, kind="ExternalInput").ap()
    w3_d = dt("w3T", [E, FT, P, KT, P], F32, kind="ExternalInput").ap()
    w2_d = dt("w2T", [KT, P, E, FT, P], F32, kind="ExternalInput").ap()
    wgf_d = dt("wgfT", [P, KT, E], F32, kind="ExternalInput").ap()
    lngam_d = dt("lngam", [P, KT], F32, kind="ExternalInput").ap()
    lnbet_d = dt("lnbet", [P, KT], F32, kind="ExternalInput").ap()
    hbeta_d = dt("hbeta", [NH, 1], F32, kind="ExternalInput").ap()
    fbias_d = dt("fbias", [E, 1], F32, kind="ExternalInput").ap()
    cexp_d = dt("cexp", [P, PT], F32, kind="ExternalInput").ap()
    out_d = dt("outfm", [D, TC], F32, kind="ExternalOutput").ap()
    x1hbm = dt("x1hbm", [D, TC], F32, kind="Internal").ap()

    A = nc.alloc_sbuf_tensor

    lngamS = A("lngamS", [P, KT], F32)
    lnbetS = A("lnbetS", [P, KT], F32)
    hbetaS = A("hbetaS", [NH, 1], F32)
    fbiasS = A("fbiasS", [E, 1], F32)
    identf = A("identf", [P, P], F32)
    iota8f = A("iota8f", [P, E], F32)
    e8f = A("e8f", [E, E * P], F32)
    onescol_bf = A("onescol_bf", [P, 1], BF16)
    onesrow_f = A("onesrow_f", [1, P], F32)
    lam16 = A("lam16", [NH, 1], F32)
    loglam16 = A("loglam16", [NH, 1], F32)
    t16 = A("t16", [NH, 1], F32)
    lamP = A("lamP", [P, NH], F32)
    loglamP = A("loglamP", [P, NH], F32)
    rowP = A("rowP", [1, NH], F32)
    fbB = A("fbB", [P, NH], F32)
    ZTf = A("ZTf", [P, KT * NH], F32)
    ZTb = A("ZTb", [P, KT * NH], BF16)
    carryS = A("carryS", [P, NH], F32)
    stateS = A("stateS", [P, NH], F32)
    wgfS = A("wgfS", [P, KT * E], BF16)
    muR = A("muR", [1, CW], F32)
    msR = A("msR", [1, CW], F32)
    rstdR = A("rstdR", [1, CW], F32)
    muB = A("muB", [P, TH], F32)
    rstdB = A("rstdB", [P, TH], F32)
    r8a = A("r8a", [E, TH], F32)
    ewTf = A("ewTf", [E, TH], F32)
    rt_m8 = A("rt_m8", [P, E], F32)
    rt_i8 = A("rt_i8", [P, E], U32)
    rt_i8f = A("rt_i8f", [P, E], F32)
    rt_oh1 = A("rt_oh1", [P, E], F32)
    rt_oh2 = A("rt_oh2", [P, E], F32)
    rt_jnk = A("rt_jnk", [P, E], F32)
    rt_w1 = A("rt_w1", [P, 1], F32)
    rt_w2 = A("rt_w2", [P, 1], F32)
    rt_ws = A("rt_ws", [P, 1], F32)
    sctS = A("sctS", [P, E], F32)
    ewtS = A("ewtS", [P, E], F32)
    ewbS = A("ewbS", [P, TH], F32)
    xnS = A("xnS", [P, KT * TH], BF16)
    xf32S = A("xf32S", [P, 3 * CW], F32)
    muBb = A("muBb", [P, TH], BF16)
    rstdBb = A("rstdBb", [P, TH], BF16)
    xcS = A("xcS", [P, 4 * CW], BF16)
    ntbS = A("ntbS", [P, 2 * CW], BF16)
    NWR = 4
    wringS = [A(f"wring{i}", [P, KT * P], BF16) for i in range(NWR)]

    psS = [nc.alloc_psum_tensor(f"ps{i}", [P, 512], F32) for i in range(8)]
    psB = [Buf() for _ in range(8)]

    muB_b = Buf()
    rstdB_b = Buf()
    rowsB = Buf()
    ewbB = Buf()
    carryB = [Buf() for _ in range(NH)]
    stateB = [Buf() for _ in range(NH)]
    wring = Ring(NWR)
    xn_b = Buf()
    vR = Ring(3)
    retR = Ring(3)
    sqR = Ring(2)
    cbR = Ring(2)
    ntR = Ring(2)
    xfR = Ring(3)
    x1R = Ring(3)
    sh1R = Ring(2)
    gtR = Ring(2)
    otR = Ring(4)
    x2B = [[Buf() for _ in range(NCW)] for _ in range(KT)]
    gbufB = [[Buf() for _ in range(FT)] for _ in range(2)]
    xcR = Ring(4)
    ntbR = Ring(2)

    def load_wcol(src_ap):
        s, b = wring.next()
        X = int(np.prod(src_ap.shape[1:]))
        tok = pr.dma("pool",
                     lambda en, t=wringS[s], sr=src_ap, X=X: en.dma_start(
                         out=t[:, 0:X], in_=sr),
                     deps=b.begin_write(), sem=f"dG{s}")
        b.wrote(tok)
        return s, b, tok

    zscope = nc.sbuf_tensor("zscope", [P, PT + 3 * PT * NH // 2 + 2 * D], F32)
    zs_t = zscope.__enter__()
    zscope.__exit__(None, None, None)
    cexp_ctx = nc.sbuf_tensor("cexpS", [P, PT], F32)
    cexpS = cexp_ctx.__enter__()
    preW_ctx = nc.sbuf_tensor("preW", [P, PT * NH], F32)
    preW = preW_ctx.__enter__()
    preWb_ctx = nc.sbuf_tensor("preWb", [P, PT * NH], BF16)
    preWb = preWb_ctx.__enter__()
    ZS_ctx = nc.sbuf_tensor("ZS", [NH, D], F32)
    ZS = ZS_ctx.__enter__()

    pr.dma("sp", lambda en: en.dma_start(out=lngamS[:], in_=lngam_d))
    pr.dma("sp", lambda en: en.dma_start(out=lnbetS[:], in_=lnbet_d))
    pr.dma("sp", lambda en: en.dma_start(out=hbetaS[:], in_=hbeta_d))
    pr.dma("sp", lambda en: en.dma_start(out=fbiasS[:], in_=fbias_d))
    pr.dma("sp", lambda en: en.dma_start(out=cexpS[:], in_=cexp_d))
    t_setup = ("dS", pr.counts["dS"])
    t_lngam = t_lnbet = t_hbeta = t_fbias = t_cexp = t_setup
    t_wgf = pr.dma("pool", lambda en: en.dma_start(
        out=wgfS[:], in_=wgf_d.rearrange("p kt e -> p (kt e)")), sem="dX")

    t_id0 = pr.tick("pool", lambda en: en.iota(
        identf[:], pattern=[[1, P]], channel_multiplier=-1,
        allow_small_or_imprecise_dtypes=True))
    t_id = pr.tick("pool", lambda en: en.tensor_scalar(
        out=identf[:], in0=identf[:], scalar1=0.0, scalar2=None,
        op0=ALU.is_equal), deps=[t_id0])
    t_io8 = pr.tick("pool", lambda en: en.iota(
        iota8f[:], pattern=[[1, E]], channel_multiplier=0,
        allow_small_or_imprecise_dtypes=True))
    t_e80 = pr.tick("pool", lambda en: en.iota(
        e8f[:], pattern=[[1, E], [0, P]], channel_multiplier=-1,
        allow_small_or_imprecise_dtypes=True))
    t_e8 = pr.tick("pool", lambda en: en.tensor_scalar(
        out=e8f[:], in0=e8f[:], scalar1=0.0, scalar2=None,
        op0=ALU.is_equal), deps=[t_e80])
    t_ones_bf = pr.tick("dve", lambda en: en.memset(onescol_bf[:], 1.0))
    t_ones_f = pr.tick("dve", lambda en: en.memset(onesrow_f[:], 1.0))

    t_lpre = pr.tick("dve", lambda en: en.tensor_scalar(
        out=t16[:], in0=hbetaS[:], scalar1=NEGLN2, scalar2=None,
        op0=ALU.mult), deps=[t_hbeta])
    t_l0 = pr.tick("act", lambda en: en.activation(
        out=t16[:], in_=t16[:], func=AF.Exp), deps=[t_lpre])
    t_l1 = pr.tick("dve", lambda en: en.tensor_scalar(
        out=lam16[:], in0=t16[:], scalar1=-1.0, scalar2=1.0,
        op0=ALU.mult, op1=ALU.add), deps=[t_l0])
    t_l2 = pr.tick("dve", lambda en: en.tensor_scalar(
        out=lam16[:], in0=lam16[:], scalar1=1.0, scalar2=TINY,
        op0=ALU.min, op1=ALU.max), deps=[t_l1])
    t_l3 = pr.tick("act", lambda en: en.activation(
        out=loglam16[:], in_=lam16[:], func=AF.Ln), deps=[t_l2])

    def bcast16(src, dst, deps, n):
        d7 = psB[7].begin_write()
        tt_ = pr.tick("pe", lambda en, s=src, n=n: en.transpose(
            out=psS[7][0:1, 0:n], in_=s[:], identity=identf[0:n, 0:n]),
            deps=list(deps) + [t_id] + d7)
        psB[7].wrote(tt_)
        te = pr.tick("dve", lambda en, n=n: en.tensor_copy(
            out=rowP[:, 0:n], in_=psS[7][0:1, 0:n]), deps=[tt_])
        psB[7].read(te)
        d7 = psB[7].begin_write()
        tm = pr.tick("pe", lambda en, n=n: en.matmul(
            out=psS[7][:, 0:n], lhsT=onesrow_f[:], rhs=rowP[:, 0:n],
            start=True, stop=True), deps=[te, t_ones_f] + d7)
        psB[7].wrote(tm)
        tb = pr.tick("dve", lambda en, d=dst, n=n: en.tensor_copy(
            out=d[:, 0:n], in_=psS[7][:, 0:n]), deps=[tm])
        psB[7].read(tb)
        return tb

    t_lamP = bcast16(lam16, lamP, [t_l2], NH)
    t_loglamP = bcast16(loglam16, loglamP, [t_l3], NH)
    t_fbB = bcast16(fbiasS, fbB, [t_fbias], E)

    t_preW = []
    for h in range(NH):
        t = pr.tick("act", lambda en, h=h: en.activation(
            out=preW[:, h:h + (PT - 1) * NH + 1:NH], in_=cexpS[:],
            func=AF.Exp, scale=loglamP[:, h:h + 1]),
            deps=[t_cexp, t_loglamP])
        t_preW.append(t)
    t_preWb = pr.tick("dve", lambda en: en.tensor_copy(
        out=preWb[:], in_=preW[:]), deps=t_preW)

    ZCH = D // 512
    z_tok = [None] * ZCH
    for j in range(PT):
        slot, wb, tld = load_wcol(xpre_d[j * P:(j + 1) * P, :])
        last = None
        for c in range(ZCH):
            dep = [tld, t_preWb]
            if j == 0:
                dep += psB[c].begin_write()
            tok = pr.step("pe", lambda en, j=j, c=c, slot=slot: en.matmul(
                out=psS[c][0:NH, 0:512],
                lhsT=preWb[:, j * NH:(j + 1) * NH],
                rhs=wringS[slot][:, c * 512:(c + 1) * 512],
                start=(j == 0), stop=(j == PT - 1)),
                deps=dep, inc="pe" if c == ZCH - 1 else None)
            if j == PT - 1:
                z_tok[c] = tok
                if c == ZCH - 1:
                    psB[c].wrote(tok)
            last = tok
        for c in range(ZCH):
            if z_tok[c] is not None and psB[c].ready is None:
                psB[c].wrote(z_tok[ZCH - 1])
        wb.read(last)
    t_Z = []
    for c in range(ZCH):
        t = pr.tick("dve", lambda en, c=c: en.tensor_copy(
            out=ZS[:, c * 512:(c + 1) * 512], in_=psS[c][0:NH, 0:512]),
            deps=[z_tok[ZCH - 1]])
        psB[c].read(t)
        t_Z.append(t)
    t_ZT = []
    for kt in range(KT):
        d4 = psB[4].begin_write()
        tt_ = pr.tick("pe", lambda en, kt=kt: en.transpose(
            out=psS[4][:, 0:NH], in_=ZS[0:NH, kt * P:(kt + 1) * P],
            identity=identf[0:NH, 0:NH]),
            deps=t_Z + [t_id] + d4)
        psB[4].wrote(tt_)
        te = pr.tick("dve", lambda en, kt=kt: en.tensor_copy(
            out=ZTf[:, kt * NH:(kt + 1) * NH], in_=psS[4][:, 0:NH]),
            deps=[tt_])
        psB[4].read(te)
        t_ZT.append(te)
    t_ZTb = pr.tick("dve", lambda en: en.tensor_copy(
        out=ZTb[:], in_=ZTf[:]), deps=t_ZT)

    ZS_ctx.__exit__(None, None, None)
    preWb_ctx.__exit__(None, None, None)
    preW_ctx.__exit__(None, None, None)
    cexp_ctx.__exit__(None, None, None)
    for b in psB:
        b.ready = None
        b.readers = []

    def ln_rows_bcast(stat_toks, bank_sum, bank_sq, bank_a, bank_b):
        toks = []
        dmu = muB_b.begin_write()
        drs = rstdB_b.begin_write()
        for c in range(NCW):
            rowdep = rowsB.begin_write()
            tmu = pr.tick("dve", lambda en, c=c: en.tensor_scalar(
                out=muR[:, 0:CW], in0=psS[bank_sum + c][0:1, 0:CW],
                scalar1=1.0 / D, scalar2=None, op0=ALU.mult),
                deps=list(stat_toks) + rowdep)
            psB[bank_sum + c].read(tmu)
            tms = pr.tick("dve", lambda en, c=c: en.tensor_scalar(
                out=msR[:, 0:CW], in0=psS[bank_sq + c][0:1, 0:CW],
                scalar1=1.0 / D, scalar2=None, op0=ALU.mult),
                deps=list(stat_toks) + rowdep)
            psB[bank_sq + c].read(tms)
            tv1 = pr.tick("dve", lambda en: en.tensor_tensor(
                out=rstdR[:, 0:CW], in0=muR[:, 0:CW], in1=muR[:, 0:CW],
                op=ALU.mult), deps=[tmu])
            tv2 = pr.tick("dve", lambda en: en.tensor_tensor(
                out=msR[:, 0:CW], in0=msR[:, 0:CW], in1=rstdR[:, 0:CW],
                op=ALU.subtract), deps=[tms, tv1])
            tve = pr.tick("dve", lambda en: en.tensor_scalar(
                out=msR[:, 0:CW], in0=msR[:, 0:CW], scalar1=EPS,
                scalar2=None, op0=ALU.add), deps=[tv2])
            tsq = pr.tick("act", lambda en: en.activation(
                out=rstdR[:, 0:CW], in_=msR[:, 0:CW], func=AF.Sqrt),
                deps=[tve])
            trc = pr.tick("dve", lambda en: en.reciprocal(
                out=rstdR[:, 0:CW], in_=rstdR[:, 0:CW]), deps=[tsq])
            da = psB[bank_a].begin_write()
            tm = pr.tick("pe", lambda en, b=bank_a: en.matmul(
                out=psS[b][:, 0:CW], lhsT=onesrow_f[:],
                rhs=muR[:, 0:CW], start=True, stop=True),
                deps=[tmu, t_ones_f] + da + dmu)
            psB[bank_a].wrote(tm)
            rowsB.read(tm)
            tbm = pr.tick("dve", lambda en, c=c, b=bank_a: en.tensor_copy(
                out=muB[:, c * CW:(c + 1) * CW], in_=psS[b][:, 0:CW]),
                deps=[tm] + dmu)
            psB[bank_a].read(tbm)
            db = psB[bank_b].begin_write()
            tr = pr.tick("pe", lambda en, b=bank_b: en.matmul(
                out=psS[b][:, 0:CW], lhsT=onesrow_f[:],
                rhs=rstdR[:, 0:CW], start=True, stop=True),
                deps=[trc, t_ones_f] + db + drs)
            psB[bank_b].wrote(tr)
            rowsB.read(tr)
            tbr = pr.tick("dve", lambda en, c=c, b=bank_b: en.tensor_copy(
                out=rstdB[:, c * CW:(c + 1) * CW], in_=psS[b][:, 0:CW]),
                deps=[tr] + drs)
            psB[bank_b].read(tbr)
            rowsB.wrote(trc)
            toks += [tbm, tbr]
            dmu = []
            drs = []
        muB_b.wrote(toks[-2])
        rstdB_b.wrote(toks[-1])
        return toks

    out_total = 0

    def emit_half(half, state_prev_half):
        nonlocal out_total
        hoff = half * TH

        with (
            nc.sbuf_tensor(f"yS{half}", [P, NH * TH], BF16) as yS,
            nc.sbuf_tensor(f"ygS{half}", [P, KT * TH], BF16) as ygS,
            nc.sbuf_tensor(f"vS{half}", [P, 3 * TH], F32) as vS,
            nc.sbuf_tensor(f"retS{half}", [P, 3 * TH], F32) as retS,
            nc.sbuf_tensor(f"sqS{half}", [P, 2 * CW], BF16) as sqS,
            nc.sbuf_tensor(f"cbS{half}", [P, 2 * CW], BF16) as cbS,
            nc.sbuf_tensor(f"ntmpS{half}", [P, 2 * CW], F32) as ntmpS,
            nc.sbuf_tensor(f"x1tS{half}", [P, 3 * CW], F32) as x1tS,
        ):
            xqS = xnS

            xn_dep = xn_b.begin_write()
            for kt in range(KT):
                t = pr.dma("pool", lambda en, kt=kt: en.dma_start(
                    out=xqS[:, kt * TH:(kt + 1) * TH],
                    in_=xfm_d[kt * P:(kt + 1) * P, hoff:hoff + TH]),
                    deps=xn_dep, sem="dX")
                xn_dep = []
            t_xq_all = ("dX", pr.counts["dX"])
            t_xq = [t_xq_all] * KT

            initS = carryS if half == 0 else stateS
            y_done = [None] * NH
            scan_done = [None] * NH
            state_save = [None] * NH

            for mt in range(NH):
                slot, wb, tld = load_wcol(wv_d[mt].rearrange("p kt m -> p (kt m)"))
                grp = None
                for kt in range(KT):
                    for c in range(NCW):
                        dep = [tld, t_xq[kt]]
                        if kt == 0:
                            dep += psB[c].begin_write()
                        grp = pr.step("pe", lambda en, kt=kt, c=c, slot=slot: en.matmul(
                            out=psS[c][:, 0:CW],
                            lhsT=wringS[slot][:, kt * P:(kt + 1) * P],
                            rhs=xqS[:, kt * TH + c * CW:kt * TH + (c + 1) * CW],
                            start=(kt == 0), stop=(kt == KT - 1)),
                            deps=dep,
                            inc="pe" if (kt == KT - 1 and c == NCW - 1 and half != 0) else None)
                if half == 0:
                    for kt in range(KT):
                        dep = [tld, t_ZTb]
                        if kt == 0:
                            dep += psB[4].begin_write()
                        grp = pr.step("pe", lambda en, kt=kt, mt=mt, slot=slot: en.matmul(
                            out=psS[4][:, mt:mt + 1],
                            lhsT=wringS[slot][:, kt * P:(kt + 1) * P],
                            rhs=ZTb[:, kt * NH + mt:kt * NH + mt + 1],
                            start=(kt == 0), stop=(kt == KT - 1)),
                            deps=dep,
                            inc="pe" if kt == KT - 1 else None)
                    psB[4].wrote(grp)
                for c in range(NCW):
                    psB[c].wrote(grp)
                wb.read(grp)
                vslot, vbuf = vR.next()
                vdep = vbuf.begin_write()
                ev = None
                for c in range(NCW):
                    ev = pr.tick("act", lambda en, vslot=vslot, c=c: en.copy(
                        out=vS[:, vslot * TH + c * CW:vslot * TH + (c + 1) * CW],
                        in_=psS[c][:, 0:CW]),
                        deps=[grp] + vdep)
                    psB[c].read(ev)
                    vdep = []
                vbuf.wrote(ev)
                if half == 0:
                    dd = carryB[mt].begin_write()
                    tce = pr.tick("dve", lambda en, mt=mt: en.tensor_copy(
                        out=carryS[:, mt:mt + 1], in_=psS[4][:, mt:mt + 1]),
                        deps=[grp] + dd)
                    psB[4].read(tce)
                    carryB[mt].wrote(tce)
                rslot, rbuf = retR.next()
                sdep = [ev, t_lamP] + rbuf.begin_write()
                if half == 0:
                    sdep.append(carryB[mt].ready)
                else:
                    sdep.append(state_prev_half[mt])
                scan_done[mt] = pr.tick("dve", lambda en, mt=mt, rslot=rslot, vslot=vslot: en.tensor_tensor_scan(
                    out=retS[:, rslot * TH:(rslot + 1) * TH],
                    data0=lamP[:, mt:mt + 1].broadcast_to([P, TH]),
                    data1=vS[:, vslot * TH:(vslot + 1) * TH],
                    initial=initS[:, mt:mt + 1],
                    op0=ALU.mult, op1=ALU.add), deps=sdep)
                vbuf.read(scan_done[mt])
                if half == 0:
                    carryB[mt].read(scan_done[mt])
                rbuf.wrote(scan_done[mt])
                sdd = stateB[mt].begin_write()
                state_save[mt] = pr.tick("dve", lambda en, mt=mt, rslot=rslot: en.tensor_copy(
                    out=stateS[:, mt:mt + 1],
                    in_=retS[:, rslot * TH + TH - 1:rslot * TH + TH]),
                    deps=[scan_done[mt]] + sdd)
                stateB[mt].wrote(state_save[mt])
                rbuf.read(state_save[mt])
                slot, wb, tld = load_wcol(wq_d[mt].rearrange("p kt m -> p (kt m)"))
                grp = None
                for kt in range(KT):
                    for c in range(NCW):
                        dep = [tld, t_xq[kt]]
                        if kt == 0:
                            dep += psB[2 + c].begin_write()
                        grp = pr.step("pe", lambda en, kt=kt, c=c, slot=slot: en.matmul(
                            out=psS[2 + c][:, 0:CW],
                            lhsT=wringS[slot][:, kt * P:(kt + 1) * P],
                            rhs=xqS[:, kt * TH + c * CW:kt * TH + (c + 1) * CW],
                            start=(kt == 0), stop=(kt == KT - 1)),
                            deps=dep,
                            inc="pe" if (kt == KT - 1 and c == NCW - 1) else None)
                for c in range(NCW):
                    psB[2 + c].wrote(grp)
                wb.read(grp)
                yd = None
                for c in range(NCW):
                    yd = pr.tick("dve", lambda en, mt=mt, c=c, rslot=rslot: en.tensor_tensor(
                        out=yS[:, mt * TH + c * CW:mt * TH + (c + 1) * CW],
                        in0=psS[2 + c][:, 0:CW],
                        in1=retS[:, rslot * TH + c * CW:rslot * TH + (c + 1) * CW],
                        op=ALU.mult),
                        deps=[grp, scan_done[mt]])
                    psB[2 + c].read(yd)
                y_done[mt] = yd
                rbuf.read(yd)

            state_next_half = state_save

            st_sum = [None] * NCW
            st_sq = [None] * NCW
            for mt in range(NH):
                for c in range(NCW):
                    sqslot, sqbuf = sqR.next()
                    tq = pr.tick("act", lambda en, mt=mt, c=c, sqslot=sqslot: en.activation(
                        out=sqS[:, sqslot * CW:(sqslot + 1) * CW],
                        in_=yS[:, mt * TH + c * CW:mt * TH + (c + 1) * CW],
                        func=AF.Square),
                        deps=[y_done[mt]] + sqbuf.begin_write())
                    sqbuf.wrote(tq)
                    dep0 = [y_done[mt], t_ones_bf]
                    if mt == 0:
                        dep0 += psB[c].begin_write()
                    st_sum[c] = pr.step("pe", lambda en, mt=mt, c=c: en.matmul(
                        out=psS[c][0:1, 0:CW], lhsT=onescol_bf[:],
                        rhs=yS[:, mt * TH + c * CW:mt * TH + (c + 1) * CW],
                        start=(mt == 0), stop=(mt == NH - 1)),
                        deps=dep0, inc="pe")
                    dep1 = [tq]
                    if mt == 0:
                        dep1 += psB[2 + c].begin_write()
                    st_sq[c] = pr.step("pe", lambda en, c=c, mt=mt, sqslot=sqslot: en.matmul(
                        out=psS[2 + c][0:1, 0:CW], lhsT=onescol_bf[:],
                        rhs=sqS[:, sqslot * CW:(sqslot + 1) * CW],
                        start=(mt == 0), stop=(mt == NH - 1)),
                        deps=dep1, inc="pe")
                    sqbuf.read(st_sq[c])
            for c in range(NCW):
                psB[c].wrote(st_sum[c])
                psB[2 + c].wrote(st_sq[c])


            bt1 = ln_rows_bcast(st_sum + st_sq, 0, 2, 4, 5)

            yg_sil = [None] * NH
            g_grp = [None] * NH
            for mt in range(NH):
                slot, wb, tld = load_wcol(wg_d[mt].rearrange("p kt m -> p (kt m)"))
                grp = None
                for kt in range(KT):
                    for c in range(NCW):
                        dep = [tld, t_xq[kt]]
                        if kt == 0:
                            dep += psB[c].begin_write()
                        grp = pr.step("pe", lambda en, kt=kt, c=c, slot=slot: en.matmul(
                            out=psS[c][:, 0:CW],
                            lhsT=wringS[slot][:, kt * P:(kt + 1) * P],
                            rhs=xqS[:, kt * TH + c * CW:kt * TH + (c + 1) * CW],
                            start=(kt == 0), stop=(kt == KT - 1)),
                            deps=dep,
                            inc="pe" if (kt == KT - 1 and c == NCW - 1) else None)
                for c in range(NCW):
                    psB[c].wrote(grp)
                wb.read(grp)
                g_grp[mt] = grp
                sil = None
                for c in range(NCW):
                    sil = pr.tick("act", lambda en, mt=mt, c=c: en.activation(
                        out=ygS[:, mt * TH + c * CW:mt * TH + (c + 1) * CW],
                        in_=psS[c][:, 0:CW], func=AF.Silu),
                        deps=[grp])
                    psB[c].read(sil)
                yg_sil[mt] = sil

            xn_b.read(g_grp[NH - 1])

            yg_done = [None] * NH
            for mt in range(NH):
                for c in range(NCW):
                    ntslot, ntbuf = ntR.next()
                    tn1 = pr.tick("dve", lambda en, mt=mt, c=c, ntslot=ntslot: en.tensor_tensor(
                        out=ntmpS[:, ntslot * CW:(ntslot + 1) * CW],
                        in0=yS[:, mt * TH + c * CW:mt * TH + (c + 1) * CW],
                        in1=muB[:, c * CW:(c + 1) * CW], op=ALU.subtract),
                        deps=[y_done[mt], muB_b.ready] + ntbuf.begin_write())
                    muB_b.read(tn1)
                    tn2 = pr.tick("dve", lambda en, mt=mt, c=c, ntslot=ntslot: en.scalar_tensor_tensor(
                        out=ntmpS[:, ntslot * CW:(ntslot + 1) * CW],
                        in0=ntmpS[:, ntslot * CW:(ntslot + 1) * CW],
                        scalar=lngamS[:, mt:mt + 1],
                        in1=rstdB[:, c * CW:(c + 1) * CW],
                        op0=ALU.mult, op1=ALU.mult),
                        deps=[tn1, t_lngam, rstdB_b.ready])
                    rstdB_b.read(tn2)
                    ygt = pr.tick("dve", lambda en, mt=mt, c=c, ntslot=ntslot: en.scalar_tensor_tensor(
                        out=ygS[:, mt * TH + c * CW:mt * TH + (c + 1) * CW],
                        in0=ntmpS[:, ntslot * CW:(ntslot + 1) * CW],
                        scalar=lnbetS[:, mt:mt + 1],
                        in1=ygS[:, mt * TH + c * CW:mt * TH + (c + 1) * CW],
                        op0=ALU.add, op1=ALU.mult),
                        deps=[tn2, yg_sil[mt], t_lnbet])
                    ntbuf.wrote(ygt)
                    ntbuf.read(ygt)
                    yg_done[mt] = ygt

            st2_sum = [None] * NCW
            st2_sq = [None] * NCW
            x1w_toks = {}
            for mt in range(NH):
                xfts = []
                for c in range(NCW):
                    xslot, xbuf = xfR.next()
                    t = pr.dma("sp", lambda en, mt=mt, c=c, xslot=xslot: en.dma_start(
                        out=xf32S[:, xslot * CW:(xslot + 1) * CW],
                        in_=xfm_d[mt * P:(mt + 1) * P, hoff + c * CW:hoff + (c + 1) * CW]),
                        deps=xbuf.begin_write(), sem=f"dR{xslot}")
                    xbuf.wrote(t)
                    xfts.append((xslot, xbuf, t))
                slot, wb, tld = load_wcol(wo_d[mt].rearrange("p kt m -> p (kt m)"))
                grp = None
                for kt in range(KT):
                    for c in range(NCW):
                        dep = [tld, yg_done[kt]]
                        if kt == 0:
                            dep += psB[2 + c].begin_write()
                        grp = pr.step("pe", lambda en, kt=kt, c=c, slot=slot: en.matmul(
                            out=psS[2 + c][:, 0:CW],
                            lhsT=wringS[slot][:, kt * P:(kt + 1) * P],
                            rhs=ygS[:, kt * TH + c * CW:kt * TH + (c + 1) * CW],
                            start=(kt == 0), stop=(kt == KT - 1)),
                            deps=dep,
                            inc="pe" if (kt == KT - 1 and c == NCW - 1) else None)
                for c in range(NCW):
                    psB[2 + c].wrote(grp)
                wb.read(grp)
                for c in range(NCW):
                    xslot, xbuf, xft = xfts[c]
                    x1slot, x1buf = x1R.next()
                    t_add = pr.tick("dve", lambda en, c=c, x1slot=x1slot, xslot=xslot: en.tensor_tensor(
                        out=x1tS[:, x1slot * CW:(x1slot + 1) * CW],
                        in0=psS[2 + c][:, 0:CW],
                        in1=xf32S[:, xslot * CW:(xslot + 1) * CW],
                        op=ALU.add),
                        deps=[grp, xft] + x1buf.begin_write())
                    psB[2 + c].read(t_add)
                    xbuf.read(t_add)
                    x1buf.wrote(t_add)
                    tw = pr.dma("sp", lambda en, mt=mt, c=c, x1slot=x1slot: en.dma_start(
                        out=x1hbm[mt * P:(mt + 1) * P, hoff + c * CW:hoff + (c + 1) * CW],
                        in_=x1tS[:, x1slot * CW:(x1slot + 1) * CW]),
                        deps=[t_add], sem=f"dW{x1slot}")
                    x1buf.read(tw)
                    x1w_toks[(mt, c)] = tw
                    cbslot, cbbuf = cbR.next()
                    tc_ = pr.tick("dve", lambda en, x1slot=x1slot, cbslot=cbslot: en.tensor_copy(
                        out=cbS[:, cbslot * CW:(cbslot + 1) * CW],
                        in_=x1tS[:, x1slot * CW:(x1slot + 1) * CW]),
                        deps=[t_add] + cbbuf.begin_write())
                    cbbuf.wrote(tc_)
                    x1buf.read(tc_)
                    sqslot, sqbuf = sqR.next()
                    tq_ = pr.tick("act", lambda en, x1slot=x1slot, sqslot=sqslot: en.activation(
                        out=sqS[:, sqslot * CW:(sqslot + 1) * CW],
                        in_=x1tS[:, x1slot * CW:(x1slot + 1) * CW],
                        func=AF.Square),
                        deps=[t_add] + sqbuf.begin_write())
                    sqbuf.wrote(tq_)
                    x1buf.read(tq_)
                    dep0 = [tc_, t_ones_bf]
                    if mt == 0:
                        dep0 += psB[4 + c].begin_write()
                    st2_sum[c] = pr.step("pe", lambda en, mt=mt, c=c, cbslot=cbslot: en.matmul(
                        out=psS[4 + c][0:1, 0:CW], lhsT=onescol_bf[:],
                        rhs=cbS[:, cbslot * CW:(cbslot + 1) * CW],
                        start=(mt == 0), stop=(mt == NH - 1)),
                        deps=dep0, inc="pe")
                    cbbuf.read(st2_sum[c])
                    dep1 = [tq_]
                    if mt == 0:
                        dep1 += psB[6 + c].begin_write()
                    st2_sq[c] = pr.step("pe", lambda en, mt=mt, c=c, sqslot=sqslot: en.matmul(
                        out=psS[6 + c][0:1, 0:CW], lhsT=onescol_bf[:],
                        rhs=sqS[:, sqslot * CW:(sqslot + 1) * CW],
                        start=(mt == 0), stop=(mt == NH - 1)),
                        deps=dep1, inc="pe")
                    sqbuf.read(st2_sq[c])
            for c in range(NCW):
                psB[4 + c].wrote(st2_sum[c])
                psB[6 + c].wrote(st2_sq[c])
            t_x1w_all = [(f"dW{k}", pr.counts[f"dW{k}"]) for k in range(4)]

            bt2 = ln_rows_bcast(st2_sum + st2_sq, 4, 6, 2, 3)

            t_mbb = pr.tick("dve", lambda en: en.tensor_copy(
                out=muBb[:, 0:TH], in_=muB[:, 0:TH]), deps=bt2 + muB_b.begin_write() * 0)
            muB_b.read(t_mbb)
            t_rbb = pr.tick("dve", lambda en: en.tensor_copy(
                out=rstdBb[:, 0:TH], in_=rstdB[:, 0:TH]), deps=bt2)
            rstdB_b.read(t_rbb)

            xn_war = g_grp[NH - 1]
            x1n_tok = {}
            for kt in range(KT):
                for c in range(NCW):
                    xcslot, xcbuf = xcR.next()
                    trb = pr.dma("pool", lambda en, kt=kt, c=c, xcslot=xcslot: en.dma_start(
                        out=xcS[:, xcslot * CW:(xcslot + 1) * CW],
                        in_=x1hbm[kt * P:(kt + 1) * P, hoff + c * CW:hoff + (c + 1) * CW]),
                        deps=t_x1w_all + xcbuf.begin_write(), sem=f"dC{xcslot}")
                    xcbuf.wrote(trb)
                    ntslot, ntbuf = ntbR.next()
                    tn1 = pr.tick("dve", lambda en, xcslot=xcslot, c=c, ntslot=ntslot: en.tensor_tensor(
                        out=ntbS[:, ntslot * CW:(ntslot + 1) * CW],
                        in0=xcS[:, xcslot * CW:(xcslot + 1) * CW],
                        in1=muBb[:, c * CW:(c + 1) * CW], op=ALU.subtract),
                        deps=[trb, t_mbb] + ntbuf.begin_write())
                    muB_b.read(tn1)
                    xcbuf.read(tn1)
                    tn2 = pr.tick("dve", lambda en, kt=kt, c=c, ntslot=ntslot: en.tensor_tensor(
                        out=xnS[:, kt * TH + c * CW:kt * TH + (c + 1) * CW],
                        in0=ntbS[:, ntslot * CW:(ntslot + 1) * CW],
                        in1=rstdBb[:, c * CW:(c + 1) * CW],
                        op=ALU.mult),
                        deps=[tn1, t_rbb, xn_war])
                    rstdB_b.read(tn2)
                    ntbuf.wrote(tn1)
                    ntbuf.read(tn2)
                    x1n_tok[(kt, c)] = tn2
                    xn_b.wrote(tn2)
            t_x1nw_all = [(f"dA{k}", pr.counts[f"dA{k}"]) for k in range(2)]

            pr.barrier()
            for b in psB:
                b.ready = None
                b.readers = []

        with (
            nc.sbuf_tensor(f"gbufS{half}", [P, 2 * FT * TH], BF16) as gbufS,
            nc.sbuf_tensor(f"x2S{half}", [P, KT * TH], F32) as x2S,
            nc.sbuf_tensor(f"sh1S{half}", [P, 2 * CW], BF16) as sh1S,
            nc.sbuf_tensor(f"gtmpS{half}", [P, 2 * CW], BF16) as gtmpS,
            nc.sbuf_tensor(f"ot2S{half}", [P, 4 * CW], F32) as ot2S,
        ):
            xrbS = xf32S
            xrR = xfR

            x1nS = xnS
            t_x1n = [x1n_tok[(kt, NCW - 1)] if (kt, NCW - 1) in x1n_tok
                     else x1n_tok[(kt, 0)] for kt in range(KT)]
            t_x1n = [None] * KT
            for kt in range(KT):
                toks = [x1n_tok[(kt, c)] for c in range(NCW)]
                t_x1n[kt] = max(toks, key=lambda t: t[1])

            rl_grp = [None] * NCW
            for kt in range(KT):
                for c in range(NCW):
                    dep = [t_wgf, t_x1n[kt]]
                    if kt == 0:
                        dep += psB[c].begin_write()
                    rl_grp[c] = pr.step("pe", lambda en, kt=kt, c=c: en.matmul(
                        out=psS[c][0:E, 0:CW],
                        lhsT=wgfS[:, kt * E:(kt + 1) * E],
                        rhs=x1nS[:, kt * TH + c * CW:kt * TH + (c + 1) * CW],
                        start=(kt == 0), stop=(kt == KT - 1)),
                        deps=dep, inc="pe" if kt == KT - 1 else None)
            for c in range(NCW):
                psB[c].wrote(rl_grp[c])
            t_rl = []
            for c in range(NCW):
                t = pr.tick("act", lambda en, c=c: en.activation(
                    out=r8a[:, c * CW:(c + 1) * CW], in_=psS[c][0:E, 0:CW],
                    func=AF.Sigmoid), deps=[rl_grp[c]])
                psB[c].read(t)
                t_rl.append(t)
            t_sc = pr.tick("dve", lambda en: en.tensor_scalar(
                out=r8a[:, 0:TH], in0=r8a[:, 0:TH], scalar1=fbiasS[:, 0:1],
                scalar2=None, op0=ALU.add), deps=t_rl + [t_fbias])
            ew_done = []
            for tt in range(TT):
                d2 = psB[2].begin_write()
                ttr = pr.tick("pe", lambda en, tt=tt: en.transpose(
                    out=psS[2][:, 0:E], in_=r8a[0:E, tt * P:(tt + 1) * P],
                    identity=identf[0:E, 0:E]),
                    deps=[t_sc, t_id] + d2)
                psB[2].wrote(ttr)
                tev = pr.tick("dve", lambda en: en.tensor_copy(
                    out=sctS[:, 0:E], in_=psS[2][:, 0:E]),
                    deps=[ttr] + ew_done[-1:])
                psB[2].read(tev)
                trw = pr.tick("dve", lambda en: en.tensor_tensor(
                    out=ewtS[:, 0:E], in0=sctS[:, 0:E], in1=fbB[:, 0:E],
                    op=ALU.subtract), deps=[tev, t_fbB])
                tm8 = pr.tick("dve", lambda en: en.max(
                    out=rt_m8[:], in_=sctS[:, 0:E]), deps=[tev])
                ti8 = pr.tick("dve", lambda en: en.max_index(
                    out=rt_i8[:], in_max=rt_m8[:], in_values=sctS[:, 0:E]),
                    deps=[tm8])
                tif = pr.tick("dve", lambda en: en.tensor_copy(
                    out=rt_i8f[:], in_=rt_i8[:]), deps=[ti8])
                to1 = pr.tick("dve", lambda en: en.tensor_scalar(
                    out=rt_oh1[:], in0=iota8f[:], scalar1=rt_i8f[:, 0:1],
                    scalar2=None, op0=ALU.is_equal), deps=[tif, t_io8])
                to2 = pr.tick("dve", lambda en: en.tensor_scalar(
                    out=rt_oh2[:], in0=iota8f[:], scalar1=rt_i8f[:, 1:2],
                    scalar2=None, op0=ALU.is_equal), deps=[tif, t_io8])
                tw1 = pr.tick("dve", lambda en: en.scalar_tensor_tensor(
                    out=rt_jnk[:], in0=rt_oh1[:], scalar=1.0,
                    in1=ewtS[:, 0:E], op0=ALU.mult, op1=ALU.mult,
                    accum_out=rt_w1[:]), deps=[to1, trw])
                tw2 = pr.tick("dve", lambda en: en.scalar_tensor_tensor(
                    out=rt_jnk[:], in0=rt_oh2[:], scalar=1.0,
                    in1=ewtS[:, 0:E], op0=ALU.mult, op1=ALU.mult,
                    accum_out=rt_w2[:]), deps=[to2, trw, tw1])
                tws = pr.tick("dve", lambda en: en.tensor_tensor(
                    out=rt_ws[:], in0=rt_w1[:], in1=rt_w2[:], op=ALU.add),
                    deps=[tw1, tw2])
                tri = pr.tick("dve", lambda en: en.reciprocal(
                    out=rt_ws[:], in_=rt_ws[:]), deps=[tws])
                tn1 = pr.tick("dve", lambda en: en.tensor_tensor(
                    out=rt_w1[:], in0=rt_w1[:], in1=rt_ws[:], op=ALU.mult),
                    deps=[tri])
                tn2 = pr.tick("dve", lambda en: en.tensor_tensor(
                    out=rt_w2[:], in0=rt_w2[:], in1=rt_ws[:], op=ALU.mult),
                    deps=[tri, tn1])
                te1 = pr.tick("dve", lambda en: en.tensor_scalar(
                    out=ewtS[:, 0:E], in0=rt_oh1[:],
                    scalar1=rt_w1[:, 0:1], scalar2=None, op0=ALU.mult),
                    deps=[tn1, trw])
                te2 = pr.tick("dve", lambda en: en.scalar_tensor_tensor(
                    out=ewtS[:, 0:E], in0=rt_oh2[:], scalar=rt_w2[:, 0:1],
                    in1=ewtS[:, 0:E], op0=ALU.mult, op1=ALU.add),
                    deps=[tn2, te1])
                d3 = psB[3].begin_write()
                ttb = pr.tick("pe", lambda en: en.transpose(
                    out=psS[3][0:E, 0:P], in_=ewtS[:, 0:E],
                    identity=identf[:, :]), deps=[te2, t_id] + d3)
                psB[3].wrote(ttb)
                tee = pr.tick("dve", lambda en, tt=tt: en.tensor_copy(
                    out=ewTf[0:E, tt * P:(tt + 1) * P], in_=psS[3][0:E, 0:P]),
                    deps=[ttb])
                psB[3].read(tee)
                ew_done.append(tee)

            for pp in range(4):
                g_done = {}
                for ei in range(2):
                    e = 2 * pp + ei
                    ewb_done = []
                    dw = ewbB.begin_write()
                    for c in range(NCW):
                        d6 = psB[6].begin_write()
                        tb = pr.tick("pe", lambda en, e=e, c=c: en.matmul(
                            out=psS[6][:, 0:CW],
                            lhsT=e8f[:, e * P:(e + 1) * P],
                            rhs=ewTf[0:E, c * CW:(c + 1) * CW],
                            start=True, stop=True),
                            deps=ew_done + [t_e8] + d6)
                        psB[6].wrote(tb)
                        te = pr.tick("dve", lambda en, c=c: en.tensor_copy(
                            out=ewbS[:, c * CW:(c + 1) * CW],
                            in_=psS[6][:, 0:CW]), deps=[tb] + dw)
                        psB[6].read(te)
                        ewb_done.append(te)
                        dw = []
                    ewbB.wrote(ewb_done[-1])
                    for ft in range(FT):
                        slot1, wb1, tld1 = load_wcol(
                            w1_d[e, ft].rearrange("p kt m -> p (kt m)"))
                        g1 = None
                        for kt in range(KT):
                            for c in range(NCW):
                                dep = [tld1, t_x1n[kt]]
                                if kt == 0:
                                    dep += psB[c].begin_write()
                                g1 = pr.step("pe", lambda en, kt=kt, c=c, slot=slot1: en.matmul(
                                    out=psS[c][:, 0:CW],
                                    lhsT=wringS[slot][:, kt * P:(kt + 1) * P],
                                    rhs=x1nS[:, kt * TH + c * CW:kt * TH + (c + 1) * CW],
                                    start=(kt == 0), stop=(kt == KT - 1)),
                                    deps=dep,
                                    inc="pe" if (kt == KT - 1 and c == NCW - 1) else None)
                        for c in range(NCW):
                            psB[c].wrote(g1)
                        wb1.read(g1)
                        slot3, wb3, tld3 = load_wcol(
                            w3_d[e, ft].rearrange("p kt m -> p (kt m)"))
                        g3 = None
                        for kt in range(KT):
                            for c in range(NCW):
                                dep = [tld3, t_x1n[kt]]
                                if kt == 0:
                                    dep += psB[2 + c].begin_write()
                                g3 = pr.step("pe", lambda en, kt=kt, c=c, slot=slot3: en.matmul(
                                    out=psS[2 + c][:, 0:CW],
                                    lhsT=wringS[slot][:, kt * P:(kt + 1) * P],
                                    rhs=x1nS[:, kt * TH + c * CW:kt * TH + (c + 1) * CW],
                                    start=(kt == 0), stop=(kt == KT - 1)),
                                    deps=dep,
                                    inc="pe" if (kt == KT - 1 and c == NCW - 1) else None)
                        for c in range(NCW):
                            psB[2 + c].wrote(g3)
                        wb3.read(g3)
                        ffn_last_g3 = g3
                        gbb = gbufB[ei][ft]
                        gdp = gbb.begin_write()
                        gd = None
                        for c in range(NCW):
                            shslot, shbuf = sh1R.next()
                            ts1 = pr.tick("act", lambda en, c=c, shslot=shslot: en.activation(
                                out=sh1S[:, shslot * CW:(shslot + 1) * CW],
                                in_=psS[c][:, 0:CW], func=AF.Silu),
                                deps=[g1] + shbuf.begin_write())
                            psB[c].read(ts1)
                            shbuf.wrote(ts1)
                            gtslot, gtbuf = gtR.next()
                            tg1 = pr.tick("dve", lambda en, c=c, shslot=shslot, gtslot=gtslot: en.tensor_tensor(
                                out=gtmpS[:, gtslot * CW:(gtslot + 1) * CW],
                                in0=psS[2 + c][:, 0:CW],
                                in1=sh1S[:, shslot * CW:(shslot + 1) * CW],
                                op=ALU.mult),
                                deps=[g3, ts1] + gtbuf.begin_write())
                            psB[2 + c].read(tg1)
                            shbuf.read(tg1)
                            gtbuf.wrote(tg1)
                            gd = pr.tick("dve", lambda en, ei=ei, ft=ft, c=c, gtslot=gtslot: en.tensor_tensor(
                                out=gbufS[:, (ei * FT + ft) * TH + c * CW:(ei * FT + ft) * TH + (c + 1) * CW],
                                in0=ewbS[:, c * CW:(c + 1) * CW],
                                in1=gtmpS[:, gtslot * CW:(gtslot + 1) * CW],
                                op=ALU.mult),
                                deps=[tg1, ewb_done[c]] + gdp)
                            ewbB.read(gd)
                            gtbuf.read(gd)
                            gdp = []
                        gbb.wrote(gd)
                        g_done[(ei, ft)] = gd
                for ht in range(KT):
                    slot2, wb2, tld2 = load_wcol(
                        w2_d[ht][:, 2 * pp:2 * pp + 2].rearrange("p e f m -> p (e f m)"))
                    grp_last = None
                    for c in range(NCW):
                        grp = None
                        for k in range(2 * FT):
                            ei, ft = divmod(k, FT)
                            dep = [tld2, g_done[(ei, ft)]]
                            if k == 0:
                                dep += psB[4 + c].begin_write()
                            grp = pr.step("pe", lambda en, k=k, c=c, slot=slot2: en.matmul(
                                out=psS[4 + c][:, 0:CW],
                                lhsT=wringS[slot][:, k * P:(k + 1) * P],
                                rhs=gbufS[:, k * TH + c * CW:k * TH + (c + 1) * CW],
                                start=(k == 0), stop=(k == 2 * FT - 1)),
                                deps=dep,
                                inc="pe" if k == 2 * FT - 1 else None)
                        psB[4 + c].wrote(grp)
                        grp_last = grp
                        x2b = x2B[ht][c]
                        if pp == 0:
                            xrslot, xrbuf = xrR.next()
                            trb = pr.dma("sp", lambda en, ht=ht, c=c, xrslot=xrslot: en.dma_start(
                                out=xrbS[:, xrslot * CW:(xrslot + 1) * CW],
                                in_=x1hbm[ht * P:(ht + 1) * P, hoff + c * CW:hoff + (c + 1) * CW]),
                                deps=t_x1w_all + xrbuf.begin_write(), sem=f"dR{xrslot}")
                            xrbuf.wrote(trb)
                            tx2 = pr.tick("dve", lambda en, ht=ht, c=c, xrslot=xrslot: en.tensor_tensor(
                                out=x2S[:, ht * TH + c * CW:ht * TH + (c + 1) * CW],
                                in0=psS[4 + c][:, 0:CW],
                                in1=xrbS[:, xrslot * CW:(xrslot + 1) * CW],
                                op=ALU.add),
                                deps=[grp, trb] + x2b.begin_write())
                            xrbuf.read(tx2)
                        else:
                            tx2 = pr.tick("dve", lambda en, ht=ht, c=c: en.scalar_tensor_tensor(
                                out=x2S[:, ht * TH + c * CW:ht * TH + (c + 1) * CW],
                                in0=psS[4 + c][:, 0:CW], scalar=1.0,
                                in1=x2S[:, ht * TH + c * CW:ht * TH + (c + 1) * CW],
                                op0=ALU.mult, op1=ALU.add),
                                deps=[grp, x2b.ready])
                        psB[4 + c].read(tx2)
                        x2b.wrote(tx2)
                    wb2.read(grp_last)
                for ei in range(2):
                    for ft in range(FT):
                        gbufB[ei][ft].read(grp_last)

            xn_b.read(ffn_last_g3)

            st3_sum = [None] * NCW
            st3_sq = [None] * NCW
            for ht in range(KT):
                for c in range(NCW):
                    x2b = x2B[ht][c]
                    cbslot, cbbuf = gtR.next()
                    tc_ = pr.tick("dve", lambda en, ht=ht, c=c, cbslot=cbslot: en.tensor_copy(
                        out=gtmpS[:, cbslot * CW:(cbslot + 1) * CW],
                        in_=x2S[:, ht * TH + c * CW:ht * TH + (c + 1) * CW]),
                        deps=[x2b.ready] + cbbuf.begin_write())
                    cbbuf.wrote(tc_)
                    x2b.read(tc_)
                    sqslot, sqbuf = sh1R.next()
                    tq_ = pr.tick("act", lambda en, ht=ht, c=c, sqslot=sqslot: en.activation(
                        out=sh1S[:, sqslot * CW:(sqslot + 1) * CW],
                        in_=x2S[:, ht * TH + c * CW:ht * TH + (c + 1) * CW],
                        func=AF.Square),
                        deps=[x2b.ready] + sqbuf.begin_write())
                    sqbuf.wrote(tq_)
                    x2b.read(tq_)
                    dep0 = [tc_, t_ones_bf]
                    if ht == 0:
                        dep0 += psB[4 + c].begin_write()
                    st3_sum[c] = pr.step("pe", lambda en, ht=ht, c=c, cbslot=cbslot: en.matmul(
                        out=psS[4 + c][0:1, 0:CW], lhsT=onescol_bf[:],
                        rhs=gtmpS[:, cbslot * CW:(cbslot + 1) * CW],
                        start=(ht == 0), stop=(ht == KT - 1)),
                        deps=dep0, inc="pe")
                    cbbuf.read(st3_sum[c])
                    dep1 = [tq_]
                    if ht == 0:
                        dep1 += psB[6 + c].begin_write()
                    st3_sq[c] = pr.step("pe", lambda en, ht=ht, c=c, sqslot=sqslot: en.matmul(
                        out=psS[6 + c][0:1, 0:CW], lhsT=onescol_bf[:],
                        rhs=sh1S[:, sqslot * CW:(sqslot + 1) * CW],
                        start=(ht == 0), stop=(ht == KT - 1)),
                        deps=dep1, inc="pe")
                    sqbuf.read(st3_sq[c])
            for c in range(NCW):
                psB[4 + c].wrote(st3_sum[c])
                psB[6 + c].wrote(st3_sq[c])

            bt3 = ln_rows_bcast(st3_sum + st3_sq, 4, 6, 2, 3)

            for ht in range(KT):
                eng3 = "dve"
                for c in range(NCW):
                    x2b = x2B[ht][c]
                    otslot, otbuf = otR.next()
                    tn1 = pr.tick(eng3, lambda en, ht=ht, c=c, otslot=otslot: en.tensor_tensor(
                        out=ot2S[:, otslot * CW:(otslot + 1) * CW],
                        in0=x2S[:, ht * TH + c * CW:ht * TH + (c + 1) * CW],
                        in1=muB[:, c * CW:(c + 1) * CW], op=ALU.subtract),
                        deps=[x2b.ready, muB_b.ready] + otbuf.begin_write())
                    muB_b.read(tn1)
                    x2b.read(tn1)
                    tn2 = pr.tick(eng3, lambda en, otslot=otslot, c=c: en.tensor_tensor(
                        out=ot2S[:, otslot * CW:(otslot + 1) * CW],
                        in0=ot2S[:, otslot * CW:(otslot + 1) * CW],
                        in1=rstdB[:, c * CW:(c + 1) * CW], op=ALU.mult),
                        deps=[tn1, rstdB_b.ready])
                    rstdB_b.read(tn2)
                    otbuf.wrote(tn2)
                    tw = pr.dma("act", lambda en, ht=ht, c=c, otslot=otslot: en.dma_start(
                        out=out_d[ht * P:(ht + 1) * P, hoff + c * CW:hoff + (c + 1) * CW],
                        in_=ot2S[:, otslot * CW:(otslot + 1) * CW]),
                        deps=[tn2], sem=f"out{otslot}")
                    otbuf.read(tw)
                    out_total += 16

            for k in range(4):
                tokk = (f"out{k}", pr.counts[f"out{k}"])
                for b in x1R.bufs:
                    b.read(tokk)
        return state_next_half

    st_half = [None] * NH
    for half in range(2):
        st_half = emit_half(half, st_half)

    dma_sems = ["dS", "dN", "dX"] + \
        [f"dG{i}" for i in range(4)] + [f"dR{i}" for i in range(4)] + \
        [f"dW{i}" for i in range(4)] + [f"dA{i}" for i in range(2)] + \
        [f"dC{i}" for i in range(4)] + [f"out{i}" for i in range(4)]
    final_deps = [(sname, pr.counts[sname]) for sname in dma_sems]
    out_deps = [(f"out{i}", pr.counts[f"out{i}"]) for i in range(4)]
    pr.step("act", lambda en: en.nop(), deps=final_deps)
    pr.step("sp", lambda en: en.nop(), deps=out_deps)

    pr.emit()
    return nc


_NCORES = 8


def _tile_attn_w(w):
    a = np.ascontiguousarray(w.T).reshape(KT, P, KT, P)
    return np.ascontiguousarray(a.transpose(2, 1, 0, 3))


def _tile_w13(w):
    a = np.ascontiguousarray(w.T).reshape(KT, P, FT, P)
    return np.ascontiguousarray(a.transpose(2, 1, 0, 3))


def _tile_w2(w):
    a = np.ascontiguousarray(w.T).reshape(FT, P, KT, P)
    return np.ascontiguousarray(a.transpose(2, 1, 0, 3))


def prep_inputs(L0_attn_x, L0_attn_Wq, L0_attn_Wv, L0_attn_Wo, L0_attn_Wg,
                L0_attn_ln_gamma, L0_attn_ln_beta, L0_attn_beta,
                L0_ffn_Wg, L0_ffn_W1, L0_ffn_W2, L0_ffn_W3, L0_ffn_bias,
                TC=2048, ncores=_NCORES):
    x = np.asarray(L0_attn_x, dtype=np.float32)
    B, L, _ = x.shape
    CPB = L // TC
    assert B * CPB == ncores
    PREF = (CPB - 1) * TC
    PT = PREF // P

    wq = _tile_attn_w(np.asarray(L0_attn_Wq, np.float32))
    wv = _tile_attn_w(np.asarray(L0_attn_Wv, np.float32))
    wg = _tile_attn_w(np.asarray(L0_attn_Wg, np.float32))
    wo = _tile_attn_w(np.asarray(L0_attn_Wo, np.float32))
    w1 = np.stack([_tile_w13(np.asarray(L0_ffn_W1[e], np.float32))
                   for e in range(E)])
    w3 = np.stack([_tile_w13(np.asarray(L0_ffn_W3[e], np.float32))
                   for e in range(E)])
    w2e = np.stack([_tile_w2(np.asarray(L0_ffn_W2[e], np.float32))
                    for e in range(E)])
    w2 = np.ascontiguousarray(w2e.transpose(1, 2, 0, 3, 4))
    wgf = np.ascontiguousarray(
        np.asarray(L0_ffn_Wg, np.float32).T.reshape(KT, P, E).transpose(1, 0, 2))
    lngam = np.ascontiguousarray(
        np.asarray(L0_attn_ln_gamma, np.float32).reshape(KT, P).T)
    lnbet = np.ascontiguousarray(
        np.asarray(L0_attn_ln_beta, np.float32).reshape(KT, P).T)
    hbeta = np.ascontiguousarray(np.asarray(L0_attn_beta, np.float32).reshape(NH, 1))
    fbias = np.ascontiguousarray(np.asarray(L0_ffn_bias, np.float32).reshape(E, 1))
    cexp = np.ascontiguousarray(
        (PREF - 1 - np.arange(PREF, dtype=np.float32)).reshape(PT, P).T)

    shared = dict(wqT=wq, wvT=wv, wgT=wg, woT=wo, w1T=w1, w3T=w3, w2T=w2,
                  wgfT=wgf, lngam=lngam, lnbet=lnbet, hbeta=hbeta,
                  fbias=fbias, cexp=cexp)

    in_maps = []
    for core in range(ncores):
        b = core // CPB
        t0 = (core % CPB) * TC
        m = dict(shared)
        m["xfm"] = np.ascontiguousarray(x[b, t0:t0 + TC, :].T)
        xpre = np.zeros((PREF, D), np.float32)
        if t0 > 0:
            xpre[PREF - t0:] = x[b, 0:t0]
        m["xpre"] = xpre
        in_maps.append(m)
    return in_maps, B, L, CPB, PREF


def kernel(**inputs):
    TC = 2048
    in_maps, B, L, CPB, PREF = prep_inputs(**inputs, TC=TC)
    nc = build(TC=TC, PREF=PREF)
    res = run_bass_kernel_spmd(nc, in_maps, core_ids=list(range(_NCORES)))
    out = np.empty((B, L, D), np.float32)
    for core in range(_NCORES):
        b = core // CPB
        t0 = (core % CPB) * TC
        out[b, t0:t0 + TC, :] = np.asarray(res.results[core]["outfm"]).T
    return out
